# version 02
# baseline (speedup 1.0000x reference)
import numpy as np
import ml_dtypes
import concourse.bass as bass
import concourse.mybir as mybir
from concourse.bass_utils import run_bass_kernel_spmd
from contextlib import ExitStack

F32 = mybir.dt.float32
BF16 = mybir.dt.bfloat16
F16 = mybir.dt.float16
AF = mybir.ActivationFunctionType
ALU = mybir.AluOpType
AX = mybir.AxisListType

NB = 2
T = 2048
E = 1024
H = 16
NTT = 16
NQT = 4
NDC = 8
NCC = 32
DHID = 4096
CEXP = 20.0

ENGS = ("pe", "act", "dve", "qin", "qout")
SEMS = ("qin_sem", "qout_sem", "pe_sem", "act_sem", "dve_sem")


def alibi_slopes():
    import math
    start = 2.0 ** (-2.0 ** (-(math.log2(H) - 3)))
    return np.array([start * start ** i for i in range(H)], dtype=np.float64)


def build_consts():
    c = {}
    c["ident_bf"] = np.eye(128, dtype=np.float16)
    c["ident_f"] = np.eye(128, dtype=np.float32)
    c["trimask"] = (np.arange(128)[:, None] <= np.arange(128)[None, :]).astype(ml_dtypes.bfloat16)
    sl = alibi_slopes()
    btab = np.zeros((128, 256), dtype=np.float64)
    for h in range(H):
        for kb in range(NTT):
            btab[:, h * 16 + kb] = -sl[h] * (kb * 128 + np.arange(128)) - CEXP
    c["btab"] = btab.astype(np.float32)
    c["ones_f"] = np.ones((128, 128), dtype=np.float32)
    return c


class MK:
    def __init__(self, debug=()):
        self.nc = bass.Bass("TRN2", target_bir_lowering=False, detect_race_conditions=False)
        self.ops = {e: [] for e in ENGS}
        self.cnt = {s: 0 for s in SEMS}
        self.wm = {e: {} for e in ENGS}
        self.debug = set(debug)
        self.dbg = {}
        self._ctx = ExitStack()
        self._mk_dram()
        self._mk_sbuf()
        nc = self.nc
        backing = self._ctx.enter_context(
            nc.sbuf_tensor("arena", [128, self.arena_total], mybir.dt.uint8))
        base = nc.lookup_mloc(backing).addr
        for h, off in self._carves:
            nc._tensor(h.name, h.shape, h.dtype, type="SB")
            mloc = nc.lookup_mloc(h)
            mloc.allocated = not nc.target_bir_lowering
            mloc.addr = base + off
        self.psa = self._ctx.enter_context(nc.psum_tensor("psa", [128, 2048], F32))
        self.psb = self._ctx.enter_context(nc.psum_tensor("psb", [128, 1536], F32))
        self.pbf = self._ctx.enter_context(nc.psum_tensor("pbf", [128, 1024], F16))
        self.sems = {s: self._ctx.enter_context(nc.semaphore(s)) for s in SEMS}

    def op(self, eng, fn, inc=None, inc_by=1):
        if inc is not None:
            self.cnt[inc] += inc_by
            def wrapped(e, sems, fn=fn, inc=inc, inc_by=inc_by):
                fn(e, sems).then_inc(sems[inc], inc_by)
            self.ops[eng].append(wrapped)
            return (inc, self.cnt[inc])
        self.ops[eng].append(lambda e, sems, fn=fn: fn(e, sems))
        return None

    def pe(self, fn, t=False):  return self.op("pe", fn, inc="pe_sem" if t else None)
    def act(self, fn, t=False): return self.op("act", fn, inc="act_sem" if t else None)
    def dve(self, fn, t=False): return self.op("dve", fn, inc="dve_sem" if t else None)

    def dsem(self, name):
        if name not in self.cnt:
            self.cnt[name] = 0
            self.sems[name] = self._ctx.enter_context(self.nc.semaphore(name))
        return name

    def dma(self, eng, out, in_, sem=None):
        sem = self.dsem(sem or ("qin_sem" if eng == "qin" else "qout_sem"))
        return self.op(eng, lambda e, s, out=out, in_=in_: e.dma_start(out=out, in_=in_),
                       inc=sem, inc_by=16)

    def wait(self, eng, *toks):
        for tok in toks:
            if tok is None:
                continue
            sem, val = tok
            if self.wm[eng].get(sem, 0) >= val:
                continue
            self.wm[eng][sem] = val
            self.ops[eng].append(lambda e, sems, sem=sem, val=val: e.wait_ge(sems[sem], val))

    def tok(self, sem):
        return (sem, self.cnt[sem])

    def join_all(self):
        return [self.tok(s) for s in list(self.cnt)]

    def _mk_dram(self):
        nc = self.nc
        self.xd = nc.dram_tensor("x", [NB, T, E], F32, kind="ExternalInput").ap()
        self.Wd = {w: nc.dram_tensor(w, [E, E], F32, kind="ExternalInput").ap()
                   for w in ("Wq", "Wk", "Wv", "Wp")}
        self.Ad = {w: nc.dram_tensor("A" + w, [128, 64], F32, kind="ExternalInput").ap() for w in "qkv"}
        self.Bd = {w: nc.dram_tensor("B" + w, [8, E], F32, kind="ExternalInput").ap() for w in "qkv"}
        self.w1d = nc.dram_tensor("w1", [DHID, E], F32, kind="ExternalInput").ap()
        self.cwd = nc.dram_tensor("conv_w", [128, 96], F32, kind="ExternalInput").ap()
        self.cbd = nc.dram_tensor("conv_b", [128, 32], F32, kind="ExternalInput").ap()
        self.w2d = nc.dram_tensor("w2", [E, DHID], F32, kind="ExternalInput").ap()
        for cname, shape, dt in (("ident_bf", [128, 128], F16), ("ident_f", [128, 128], F32),
                                 ("trimask", [128, 128], BF16), ("btab", [128, 256], F32),
                                 ("ones_f", [128, 128], F32)):
            setattr(self, cname + "_d", nc.dram_tensor(cname, shape, dt, kind="ExternalInput").ap())
        self.out_d = nc.dram_tensor("out", [NB, T, E], F32, kind="ExternalOutput").ap()
        scr_kind = "ExternalOutput" if "scr" in self.debug else "Internal"
        self.w1qT_d = nc.dram_tensor("w1qT_scr", [NCC, 128, 1024], F16, kind=scr_kind).ap()
        self.w2qT_d = nc.dram_tensor("w2qT_scr", [2, NCC, 128, 512], F16, kind=scr_kind).ap()

    def dbg_out(self, name, shape, dtype=F32):
        if name in self.debug and name not in self.dbg:
            self.dbg[name] = self.nc.dram_tensor("dbg_" + name, list(shape), dtype,
                                                 kind="ExternalOutput").ap()
        return self.dbg.get(name)

    def _mk_sbuf(self):
        nc = self.nc
        self.arena_total = 205 * 1024
        self._carves = []
        cur = [0]

        def place(name, shape, dtype, at=None):
            per = int(np.prod(shape[1:])) * mybir.dt.size(dtype)
            o = cur[0] if at is None else at
            assert o % 32 == 0 and o + per <= self.arena_total, (name, o, per)
            h = nc.virtual_sbuf_tensor(name, list(shape), dtype)
            self._carves.append((h, o))
            if at is None:
                cur[0] = (o + per + 31) & ~31
            return h

        self.identb = place("identb", [128, 128], F16)
        self.identf = place("identf", [128, 128], F32)
        self.trim = place("trim", [128, 128], BF16)
        self.btab = place("btabs", [128, 256], F32)
        self.ones = place("oness", [128, 128], F32)
        self.cwb = place("cwb", [128, NCC * 4], F32)
        self.sca = place("sca", [128, 32], F32)
        self.spbuf = place("spbuf", [128, 256], F32)
        self.BT = {w: place("B%sT" % w, [128, NDC * 8], F16) for w in "qkv"}
        self.AT = {w: place("A%sT" % w, [128, E], F16) for w in "qkv"}
        self.xqT = place("xqT", [128, NDC * T], F16)
        self.B_OFF = cur[0]
        self.xnT = place("xnT", [128, NDC * T], F16)
        self.C_OFF = cur[0]
        self.QT = place("QT", [128, 8 * T], F16)
        self.KT = place("KT", [128, 8 * T], F16)
        self.Vsb = place("Vsb", [128, NTT * H * 65], BF16)
        self.attnNT = place("attnNT", [128, 8 * 512], F16)
        self.Pbuf = place("Pbuf", [128, 4 * 512], BF16)
        a_off = self._off(self.attnNT)
        self.uT = {w: place("uT" + w, [128, T], F16, at=a_off + i * (T * 2))
                   for i, w in enumerate("qkv")}
        self.C_END = cur[0]
        self.Wstage = place("Wstage", [128, 8192], F16)
        self.DX_OFF = cur[0]
        self.Wtmp = [place("Wtmp0", [128, 512], F32), place("Wtmp1", [128, 512], F32)]
        self.Wcast = place("Wcast", [128, 1024], F16)
        self.D_END = cur[0]
        assert cur[0] <= self.arena_total, cur[0]
        self.rsrow = place("rsrow", [128, 512], F32, at=self._off(self.Wcast))

        o = self._off(self.Wstage)
        def placeE(name, shape, dtype):
            nonlocal o
            per = int(np.prod(shape[1:])) * mybir.dt.size(dtype)
            h = place(name, shape, dtype, at=o)
            o = (o + per + 31) & ~31
            return h
        self.xbuf = [placeE("xbuf0", [128, E], F32), placeE("xbuf1", [128, E], F32)]
        self.xnf = placeE("xnf", [128, E], F32)
        self.xnb = placeE("xnb", [128, E], F16)
        self.xqtok = placeE("xqtok", [128, E], F16)
        self.i16st = placeE("i16st", [128, E], F16)
        self.lnst = [placeE("lnst0", [128, 16], F32), placeE("lnst1", [128, 16], F32)]
        assert o <= self._off(self.Wcast), (o, self._off(self.Wcast))

        o = self.B_OFF
        def placeF(name, shape, dtype):
            nonlocal o
            per = int(np.prod(shape[1:])) * mybir.dt.size(dtype)
            h = place(name, shape, dtype, at=o)
            o = (o + per + 31) & ~31
            return h
        self.h2buf = placeF("h2buf", [128, NCC * 512], F32)
        self.Icache = placeF("Icache", [128, NCC * 512], F16)
        self.w1st = [placeF("w1st0", [128, 1024], F16), placeF("w1st1", [128, 1024], F16)]
        self.w2st = [placeF("w2st0", [128, 512], F16), placeF("w2st1", [128, 512], F16)]
        self.resst = [placeF("resst%d" % i, [128, E], F32) for i in range(4)]
        self.outst = [placeF("outst0", [128, E], F32), placeF("outst1", [128, E], F32)]
        self.yst = placeF("yst", [128, 512], F32)
        self.i16f = placeF("i16f", [128, 512], F16)
        self.h2pre = [placeF("h2pre0", [128, 512], F32), placeF("h2pre1", [128, 512], F32)]
        self.repsh = placeF("repsh", [128, 512], F32)
        self.Mmax = placeF("Mmax", [128, 512], F32)
        self.halo = placeF("halo", [128, NCC], F32)
        self.fcols = placeF("fcols", [128, 64], F32)
        self.srowF = placeF("srowF", [128, 512], F32)
        assert o <= self.C_END, (o, self.C_END)

        o = self.B_OFF
        def placeP3(name, shape, dtype):
            nonlocal o
            per = int(np.prod(shape[1:])) * mybir.dt.size(dtype)
            h = place(name, shape, dtype, at=o)
            o = (o + per + 31) & ~31
            return h
        self.p3res = [placeP3("p3res%d" % i, [128, E], F32) for i in range(4)]
        self.p3out = [placeP3("p3out%d" % i, [128, E], F32) for i in range(2)]
        self.anrm = [placeP3("anrm%d" % i, [128, 512], F32) for i in range(2)]
        assert o <= self.C_OFF + 4096, o

    def _off(self, h):
        for hh, o in self._carves:
            if hh is h:
                return o
        raise KeyError

    def finalize(self):
        nc = self.nc
        sems = self.sems
        ops = self.ops
        with self._ctx:
            with nc.Block() as block:
                block.tensor(lambda e: [f(e, sems) for f in ops["pe"]])
                block.scalar(lambda e: [f(e, sems) for f in ops["act"]])
                block.vector(lambda e: [f(e, sems) for f in ops["dve"]])
                block.sync(lambda e: [f(e, sems) for f in ops["qin"]])
                block.gpsimd(lambda e: [f(e, sems) for f in ops["qout"]])
        return nc

    def psa_(self, c0, c1, p0=0, p1=128):
        return lambda: self.psa[p0:p1, c0:c1]


def lap(f):
    return f



def _init_state(M):
    M.pbf_slot = 0
    M.t_pbf_group = []
    M.t_wtmp = [None, None]
    M.t_wcast = None
    M.t_xbuf = [None] * 2
    M.t_lnst = [None] * 2


def tr_group(M, items, rows=128):
    n = len(items)
    assert n <= 8
    M.wait("pe", *getattr(M, "t_pbf_group", []))
    for j, (src_fn, eng, fn) in enumerate(items):
        tp = M.pe(lambda e, s, j=j, src_fn=src_fn: e.transpose(
            M.pbf[0:rows, j * 128:(j + 1) * 128], src_fn(), M.identb[:]), t=(j == n - 1))
    toks = {}
    engs = []
    for eng in ("dve", "act"):
        idxs = [j for j, it in enumerate(items) if it[1] == eng]
        if not idxs:
            continue
        M.wait(eng, tp)
        for k, j in enumerate(idxs):
            fn = items[j][2]
            tv = M.op(eng, lambda e, s, j=j, fn=fn: fn(e, M.pbf[:, j * 128:(j + 1) * 128]),
                      inc={"dve": "dve_sem", "act": "act_sem"}[eng] if k == len(idxs) - 1 else None)
        toks[eng] = tv
    M.t_pbf_group = [t for t in toks.values()]
    return toks


def dve_spacer(M, t=False):
    M.dve(lambda e, s: e.tensor_copy(out=M.spbuf[:], in_=M.btab[:]))
    return M.dve(lambda e, s: e.tensor_copy(out=M.spbuf[:], in_=M.btab[:]), t=t)


def p0_prep(M, parts=("lora", "means", "w1", "w2")):
    for dst, src in ((M.identb, M.ident_bf_d), (M.identf, M.ident_f_d), (M.trim, M.trimask_d),
                     (M.btab, M.btab_d), (M.ones, M.ones_f_d)):
        t_c = M.dma("qin", dst[:], src[:], sem="s_c")
    M.wait("pe", t_c)
    M.wait("dve", t_c)
    M.wait("act", t_c)
    M.dve(lambda e, s: e.tensor_scalar(out=M.sca[:, 16:17], in0=M.ones[:, 0:1],
                                       scalar1=1e-5, scalar2=None, op0=ALU.mult))
    for i4 in range(4):
        tzf = M.dve(lambda e, s, i4=i4: e.tensor_scalar(
            out=M.Wtmp[0][:, i4 * 128:(i4 + 1) * 128], in0=M.ones[:],
            scalar1=0.0, scalar2=None, op0=ALU.mult), t=(i4 == 3))
    M.t_wtmp[0] = tzf

    for w in ("qkv" if "lora" in parts else ""):
        M.wait("dve", M.t_wtmp[0], M.t_wcast)
        tc = M.tok("dve_sem")
        for half in range(2):
            M.wait("qin", tc)
            tl = M.dma("qin", M.Wtmp[0][0:8, :], M.Bd[w][:, half * 512:(half + 1) * 512], sem="s_wt0")
            M.wait("dve", tl)
            tc = M.dve(lambda e, s, half=half: e.tensor_copy(
                out=M.Wcast[:, half * 512:(half + 1) * 512], in_=M.Wtmp[0][:]), t=True)
        M.t_wtmp[0] = tc
        M.wait("pe", tc)
        toks = tr_group(M, [
            (lambda dc=dc: M.Wcast[:, dc * 128:(dc + 1) * 128], "dve",
             lambda e, p, dc=dc, w=w: e.tensor_copy(out=M.BT[w][:, dc * 8:(dc + 1) * 8], in_=p[:, 0:8]))
            for dc in range(NDC)])
        M.t_wcast = toks["dve"]

    for w in ("qkv" if "lora" in parts else ""):
        M.wait("qin", M.t_wtmp[0])
        tl = M.dma("qin", M.Wtmp[0][:, 0:64], M.Ad[w][:], sem="s_wt0")
        M.wait("dve", tl, M.t_wcast)
        tc = M.dve(lambda e, s: e.tensor_copy(out=M.Wcast[:, 0:64], in_=M.Wtmp[0][:, 0:64]), t=True)
        M.t_wtmp[0] = tc
        M.wait("pe", tc)
        toks = tr_group(M, [
            (lambda dc=dc: M.Wcast[:, dc * 8:(dc + 1) * 8], "act",
             lambda e, p, dc=dc, w=w: e.activation(
                 out=M.AT[w][0:8, dc * 128:(dc + 1) * 128], in_=p[0:8, :], func=AF.Copy, scale=2.0))
            for dc in range(NDC)], rows=8)
        M.t_wcast = toks["act"]

    def wsum(wap, nch, c0):
        for i in range(nch * 2):
            b = i % 2
            M.wait("qin", M.t_wtmp[b])
            r0 = (i // 2) * 128
            tl = M.dma("qin", M.Wtmp[b][:], wap[r0:r0 + 128, (i % 2) * 512:(i % 2) * 512 + 512], sem="s_wt%d" % b)
            M.wait("dve", tl)
            tr = M.dve(lambda e, s, c0=c0, i=i, b=b: e.tensor_reduce(
                out=M.cwb[:, c0 + i:c0 + i + 1], in_=M.Wtmp[b][:], axis=AX.X, op=ALU.add,
                apply_absolute_value=True), t=True)
            M.t_wtmp[b] = tr
        return tr

    if "means" not in parts:
        return
    wsum(M.w1d, NCC, 0)
    w2flat = M.w2d[:].rearrange("e (g c) -> (e g) c", g=4)
    wsum(w2flat, NCC, 64)
    t12 = dve_spacer(M, t=True)
    M.wait("pe", t12)
    tb = M.pe(lambda e, s: e.matmul(M.psb[:, 0:128], M.ones[:, 0:128], M.cwb[:, 0:128],
                                    start=True, stop=True), t=True)
    M.wait("dve", tb)
    M.dve(lambda e, s: e.tensor_reduce(out=M.sca[:, 8:9], in_=M.psb[:, 0:64],
                                       axis=AX.X, op=ALU.add))
    M.dve(lambda e, s: e.tensor_reduce(out=M.sca[:, 9:10], in_=M.psb[:, 64:128],
                                       axis=AX.X, op=ALU.add))
    dve_spacer(M)
    M.dve(lambda e, s: e.tensor_scalar(out=M.sca[:, 12:13], in0=M.sca[:, 8:9],
                                       scalar1=1.0 / (DHID * E), scalar2=1e-5, op0=ALU.mult, op1=ALU.max))
    M.dve(lambda e, s: e.tensor_scalar(out=M.sca[:, 13:14], in0=M.sca[:, 9:10],
                                       scalar1=1.0 / (DHID * E), scalar2=1e-5, op0=ALU.mult, op1=ALU.max))
    dve_spacer(M)
    M.dve(lambda e, s: e.reciprocal(out=M.sca[:, 14:15], in_=M.sca[:, 12:13]))
    t4 = M.dve(lambda e, s: e.reciprocal(out=M.sca[:, 15:16], in_=M.sca[:, 13:14]), t=True)
    dve_spacer(M)

    M.wait("qin", M.t_wtmp[1])
    M.dma("qin", M.Wtmp[1][:, 0:96], M.cwd[:], sem="s_wt1")
    t_cb = M.dma("qin", M.Wtmp[1][:, 96:128], M.cbd[:], sem="s_wt1")
    M.wait("dve", t_cb)
    for cc in range(NCC):
        M.dve(lambda e, s, cc=cc: e.tensor_scalar(
            out=M.cwb[:, cc * 4:cc * 4 + 3], in0=M.Wtmp[1][:, cc * 3:cc * 3 + 3],
            scalar1=M.sca[:, 12:13], scalar2=None, op0=ALU.mult))
        t5 = M.dve(lambda e, s, cc=cc: e.tensor_copy(
            out=M.cwb[:, cc * 4 + 3:cc * 4 + 4], in_=M.Wtmp[1][:, 96 + cc:97 + cc]), t=True)
    M.t_wtmp[1] = t5

    def ternarize(buf, scol, half):
        c0 = half * 512
        M.dve(lambda e, s, buf=buf, scol=scol: e.tensor_scalar(
            out=M.repsh[:], in0=buf[:], scalar1=M.sca[:, scol:scol + 1],
            scalar2=0.5, op0=ALU.mult, op1=ALU.is_ge))
        M.dve(lambda e, s, buf=buf, scol=scol: e.tensor_scalar(
            out=M.yst[:], in0=buf[:], scalar1=M.sca[:, scol:scol + 1],
            scalar2=-0.5, op0=ALU.mult, op1=ALU.is_le))
        return M.dve(lambda e, s, c0=c0: e.tensor_tensor(
            out=M.Wcast[:, c0:c0 + 512], in0=M.repsh[:], in1=M.yst[:],
            op=ALU.subtract), t=True)

    t_store = None
    for cc in range(NCC if "w1" in parts else 0):
        for half in range(2):
            b = half
            M.wait("qin", M.t_wtmp[b])
            tl = M.dma("qin", M.Wtmp[b][:], M.w1d[cc * 128:(cc + 1) * 128, half * 512:(half + 1) * 512], sem="s_wt%d" % b)
            M.wait("dve", tl, M.t_wcast, t_store)
            tq = ternarize(M.Wtmp[b], 14, half)
            M.t_wtmp[b] = tq
        M.wait("pe", tq, *getattr(M, "t_pbf_group", []))
        for dc in range(NDC):
            tp = M.pe(lambda e, s, dc=dc: e.transpose(
                M.pbf[:, dc * 128:(dc + 1) * 128], M.Wcast[:, dc * 128:(dc + 1) * 128],
                M.identb[:]), t=(dc == NDC - 1))
        M.wait("dve", tp, t_store)
        tev = M.dve(lambda e, s: e.tensor_copy(out=M.Wcast[:, 0:1024], in_=M.pbf[:, 0:1024]), t=True)
        M.t_wcast = tev
        M.t_pbf_group = [tev]
        M.wait("qout", tev)
        t_store = M.dma("qout", M.w1qT_d[cc, :, :], M.Wcast[:, 0:1024], sem="s_scr")

    for ec in range(8 if "w2" in parts else 0):
        eh = ec // 4
        for sub in range(4):
            for half in range(2):
                b = half
                M.wait("qin", M.t_wtmp[b])
                c_lo = sub * 1024 + half * 512
                tl = M.dma("qin", M.Wtmp[b][:], M.w2d[ec * 128:(ec + 1) * 128, c_lo:c_lo + 512], sem="s_wt%d" % b)
                M.wait("dve", tl, M.t_wcast, t_store)
                tq = ternarize(M.Wtmp[b], 15, half)
                M.t_wtmp[b] = tq
            M.wait("pe", tq, *getattr(M, "t_pbf_group", []))
            for k in range(NDC):
                tp = M.pe(lambda e, s, k=k: e.transpose(
                    M.pbf[:, k * 128:(k + 1) * 128], M.Wcast[:, k * 128:(k + 1) * 128],
                    M.identb[:]), t=(k == NDC - 1))
            M.wait("dve", tp, t_store)
            tev = M.dve(lambda e, s: e.tensor_copy(out=M.Wcast[:, 0:1024], in_=M.pbf[:, 0:1024]), t=True)
            M.t_wcast = tev
            M.t_pbf_group = [tev]
            M.wait("qout", tev)
            for k in range(8):
                cc = sub * 8 + k
                t_store = M.dma("qout", M.w2qT_d[eh, cc, :, (ec % 4) * 128:(ec % 4 + 1) * 128],
                                M.Wcast[:, k * 128:(k + 1) * 128], sem="s_scr")


def phase_join(M):
    toks = [M.tok(s) for s in list(M.cnt)]
    for e in ENGS:
        for t in toks:
            M.wait(e, t)


def ln_phase(M, s):
    t_xnf = None
    t_i16 = None
    t_xq = None
    t_xnb = None
    t_xnb_tr = None
    t_xq_tr = None
    for tt in range(NTT):
        b = tt % 2
        M.wait("qin", M.t_xbuf[b])
        tl = M.dma("qin", M.xbuf[b][:], M.xd[s, tt * 128:(tt + 1) * 128, :], sem="s_xb%d" % b)
        st = M.lnst[b]
        M.wait("dve", tl, M.t_lnst[b])
        M.dve(lambda e, sm, b=b, st=st: e.tensor_reduce(
            out=st[:, 0:1], in_=M.xbuf[b][:], axis=AX.X, op=ALU.add))
        M.dve(lambda e, sm, b=b: e.tensor_tensor(
            out=M.psa[:, 0:1024], in0=M.xbuf[b][:], in1=M.xbuf[b][:], op=ALU.mult))
        t_sq = M.dve(lambda e, sm, st=st: e.tensor_reduce(
            out=st[:, 1:2], in_=M.psa[:, 0:1024], axis=AX.X, op=ALU.add), t=True)
        M.wait("act", t_sq)
        M.act(lambda e, sm, st=st: e.activation(out=st[:, 9:10], in_=st[:, 0:1],
              func=AF.Square, scale=1.0 / E))
        t_m = M.act(lambda e, sm, st=st: e.activation(out=st[:, 2:3], in_=st[:, 0:1],
              func=AF.Copy, scale=-1.0 / E), t=True)
        M.wait("dve", t_m)
        t_var = M.dve(lambda e, sm, st=st: e.scalar_tensor_tensor(
            out=st[:, 3:4], in0=st[:, 1:2], scalar=1.0 / E, in1=st[:, 9:10],
            op0=ALU.mult, op1=ALU.subtract), t=True)
        M.wait("act", t_var)
        t_sqr = M.act(lambda e, sm, st=st: e.activation(out=st[:, 10:11], in_=st[:, 3:4],
              func=AF.Sqrt, bias=M.sca[:, 16:17]), t=True)
        M.wait("dve", t_sqr)
        t_rstd = M.dve(lambda e, sm, st=st: e.reciprocal(out=st[:, 4:5], in_=st[:, 10:11]), t=True)
        M.wait("act", t_rstd)
        M.act(lambda e, sm, st=st: e.activation(out=st[:, 5:6], in_=st[:, 4:5],
              func=AF.Copy, scale=st[:, 2:3]))
        t_xn0 = M.act(lambda e, sm, b=b, st=st: e.activation(
            out=M.psa[:, 1024:2048], in_=M.xbuf[b][:], func=AF.Identity,
            scale=st[:, 4:5]), t=True)
        M.t_xbuf[b] = t_xn0
        M.wait("dve", t_xn0, t_xnf)
        t_xn = M.dve(lambda e, sm, st=st: e.tensor_scalar(
            out=M.xnf[:], in0=M.psa[:, 1024:2048], scalar1=st[:, 5:6], scalar2=None,
            op0=ALU.add), t=True)
        t_max = M.dve(lambda e, sm, st=st: e.tensor_reduce(out=st[:, 6:7], in_=M.xnf[:],
              axis=AX.X, op=ALU.max, apply_absolute_value=True), t=True)
        dve_spacer(M)
        t_clip = M.dve(lambda e, sm, st=st: e.tensor_scalar(out=st[:, 8:9], in0=st[:, 6:7],
              scalar1=1e-5, scalar2=1.0 / 127.0, op0=ALU.max, op1=ALU.mult), t=True)
        M.wait("act", t_clip)
        t_hop = M.act(lambda e, sm, st=st: e.activation(out=st[:, 12:13], in_=st[:, 8:9],
              func=AF.Copy), t=True)
        M.wait("dve", t_hop)
        t_sx = M.dve(lambda e, sm, st=st: e.reciprocal(out=st[:, 13:14], in_=st[:, 12:13]), t=True)
        dve_spacer(M)
        t_i16 = M.dve(lambda e, sm, st=st: e.tensor_scalar(
            out=M.i16st[:], in0=M.xnf[:], scalar1=st[:, 13:14], scalar2=1536.0,
            op0=ALU.mult, op1=ALU.add), t=True)
        M.wait("dve", t_xq_tr)
        t_xq = M.dve(lambda e, sm, st=st: e.tensor_scalar(
            out=M.xqtok[:], in0=M.i16st[:], scalar1=-1536.0, scalar2=st[:, 8:9],
            op0=ALU.add, op1=ALU.mult), t=True)
        M.t_lnst[b] = t_xq
        if "lnstats" in M.debug:
            M.dve(lambda e, sm, st=st, tt=tt: e.tensor_copy(
                out=M.resst[tt // 4][:, (tt % 4) * 16:(tt % 4) * 16 + 16], in_=st[:]))
        M.wait("act", t_xn, t_xnb_tr)
        t_xnb = M.act(lambda e, sm: e.activation(out=M.xnb[:], in_=M.xnf[:], func=AF.Copy), t=True)
        t_xnf = t_xnb
        M.wait("pe", t_xnb)
        toks = tr_group(M, [
            (lambda dc=dc: M.xnb[:, dc * 128:(dc + 1) * 128], "act",
             lambda e, p, dc=dc, tt=tt: e.activation(
                 out=M.xnT[:, dc * T + tt * 128:dc * T + (tt + 1) * 128], in_=p[:], func=AF.Copy))
            for dc in range(NDC)])
        t_xnb_tr = toks["act"]
        M.wait("pe", t_xq)
        toks = tr_group(M, [
            (lambda dc=dc: M.xqtok[:, dc * 128:(dc + 1) * 128], "dve",
             lambda e, p, dc=dc, tt=tt: e.tensor_copy(
                 out=M.xqT[:, dc * T + tt * 128:dc * T + (tt + 1) * 128], in_=p[:]))
            for dc in range(NDC)])
        t_xq_tr = toks["dve"]
    M.t_xnT_done = t_xnb_tr
    M.t_xqT_done = t_xq_tr


def prep_WT(M, wname, t_stage_free):
    tv = None
    M.wait("act", t_stage_free)
    for ec in range(8):
        for half in range(2):
            b = half
            M.wait("qin", M.t_wtmp[b])
            tl = M.dma("qin", M.Wtmp[b][:],
                       M.Wd[wname][ec * 128:(ec + 1) * 128, half * 512:(half + 1) * 512], sem="s_wt%d" % b)
            M.wait("dve", tl, M.t_wcast if half == 0 else None)
            tc = M.dve(lambda e, s, b=b, half=half: e.tensor_copy(
                out=M.Wcast[:, half * 512:(half + 1) * 512], in_=M.Wtmp[b][:]), t=True)
            M.t_wtmp[b] = tc
        M.wait("pe", tc)
        toks = tr_group(M, [
            (lambda dc=dc: M.Wcast[:, dc * 128:(dc + 1) * 128], "act",
             lambda e, p, dc=dc, ec=ec: e.activation(
                 out=M.Wstage[:, dc * 1024 + ec * 128:dc * 1024 + (ec + 1) * 128],
                 in_=p[:], func=AF.Copy))
            for dc in range(NDC)])
        tv = toks["act"]
        M.t_wcast = tv
    return tv


def qkv_phase(M, s):
    t_uslot = [None, None]
    ucols = [(0, 512), (512, 1024)]
    i = 0
    t_xn_ok = M.t_xnT_done
    for w in "qkv":
        for tg in range(4):
            sl_ = i % 2
            i += 1
            c0, c1 = ucols[sl_]
            M.wait("pe", t_uslot[sl_], t_xn_ok)
            for dc in range(NDC):
                tm = M.pe(lambda e, sm, w=w, dc=dc, tg=tg, c0=c0, c1=c1: e.matmul(
                    M.psb[0:8, c0:c1], M.BT[w][:, dc * 8:(dc + 1) * 8],
                    M.xnT[:, dc * T + tg * 512:dc * T + tg * 512 + 512],
                    start=(dc == 0), stop=(dc == NDC - 1)), t=(dc == NDC - 1))
            M.wait("dve", tm)
            t_ev = M.dve(lambda e, sm, w=w, tg=tg, c0=c0, c1=c1: e.tensor_copy(
                out=M.uT[w][0:8, tg * 512:tg * 512 + 512], in_=M.psb[0:8, c0:c1]), t=True)
            t_uslot[sl_] = t_ev
    t_u_done = t_ev

    t_stage_free = getattr(M, "t_wstage_use", None)
    t_psa = [None] * 4
    for w, dst in (("Wq", M.QT), ("Wk", M.KT)):
        t_prep = prep_WT(M, w, t_stage_free)
        M.wait("pe", t_prep, t_u_done)
        for ec in range(8):
            M.wait("pe", *t_psa)
            for dc in range(NDC):
                for tg in range(4):
                    M.pe(lambda e, sm, w=w, ec=ec, dc=dc, tg=tg: e.matmul(
                        M.psa[:, tg * 512:(tg + 1) * 512],
                        M.Wstage[:, dc * 1024 + ec * 128:dc * 1024 + (ec + 1) * 128],
                        M.xnT[:, dc * T + tg * 512:dc * T + tg * 512 + 512],
                        start=(dc == 0), stop=False))
            wl = "q" if w == "Wq" else "k"
            for tg in range(4):
                tm = M.pe(lambda e, sm, wl=wl, ec=ec, tg=tg: e.matmul(
                    M.psa[:, tg * 512:(tg + 1) * 512],
                    M.AT[wl][0:8, ec * 128:(ec + 1) * 128],
                    M.uT[wl][0:8, tg * 512:tg * 512 + 512],
                    start=False, stop=True), t=True)
                M.wait("act", tm)
                t_psa[tg] = M.act(lambda e, sm, dst=dst, ec=ec, tg=tg: e.activation(
                    out=dst[:, ec * T + tg * 512:ec * T + tg * 512 + 512],
                    in_=M.psa[:, tg * 512:(tg + 1) * 512], func=AF.Copy), t=True)
        t_stage_free = M.tok("pe_sem")

    t_prep = prep_WT(M, "Wv", t_stage_free)
    M.wait("pe", t_prep)
    t_vslot = [None] * 3
    vcols = [(0, 512), (512, 1024), (1024, 1536)]
    i = 0
    t_ones = None
    for tt in range(NTT):
        for eh in range(2):
            sl_ = i % 3
            i += 1
            c0, c1 = vcols[sl_]
            M.wait("pe", t_vslot[sl_])
            for dc in range(NDC):
                M.pe(lambda e, sm, tt=tt, eh=eh, dc=dc, c0=c0, c1=c1: e.matmul(
                    M.psb[:, c0:c1],
                    M.xnT[:, dc * T + tt * 128:dc * T + (tt + 1) * 128],
                    M.Wstage[:, dc * 1024 + eh * 512:dc * 1024 + eh * 512 + 512],
                    start=(dc == 0), stop=False))
            tm = M.pe(lambda e, sm, tt=tt, eh=eh, c0=c0, c1=c1: e.matmul(
                M.psb[:, c0:c1],
                M.uT["v"][0:8, tt * 128:(tt + 1) * 128],
                M.AT["v"][0:8, eh * 512:eh * 512 + 512],
                start=False, stop=True), t=True)
            M.wait("dve", tm)
            def vout(tt, eh):
                a = tt * 1040 + eh * 520
                return M.Vsb[:, a:a + 520].rearrange("p (h c) -> p h c", h=8)[:, :, 0:64]
            t_vslot[sl_] = M.dve(lambda e, sm, tt=tt, eh=eh, c0=c0, c1=c1: e.tensor_copy(
                out=vout(tt, eh), in_=M.psb[:, c0:c1].rearrange("p (h c) -> p h c", h=8)), t=True)
        t_ones = M.dve(lambda e, sm, tt=tt: e.tensor_copy(
            out=M.Vsb[:, tt * 1040:(tt + 1) * 1040].rearrange("p (h c) -> p h c", h=16)[:, :, 64:65],
            in_=M.ones[:, 0:16].rearrange("p (h c) -> p h c", h=16)), t=True)
    M.t_wstage_use = M.tok("pe_sem")
    M.t_v_done = (t_vslot, t_ones)


def attn_phase(M, s):
    t_prep = prep_WT(M, "Wp", M.t_wstage_use)
    M.wait("pe", t_prep)
    t_S = [None] * 4
    t_P = [None] * 4
    t_pv = [None, None]
    t_rep = None
    t_wp = [None, None, None]
    t_p3res = [None] * 4
    t_p3out = [None, None]
    M.t_res_tok = getattr(M, "t_res_tok", {})
    for qt in range(NQT):
        for hp in range(8):
            last = 4 * qt + 3
            for kb in range(4 * qt + 4):
                par = kb % 2
                diag = kb - 4 * qt
                cs = diag * 128 if diag >= 0 else 0
                SA = (par * 2) * 512
                SB = (par * 2 + 1) * 512
                M.wait("pe", t_S[par * 2], t_S[par * 2 + 1])
                tmA = M.pe(lambda e, sm, hp=hp, qt=qt, kb=kb, cs=cs, SA=SA: e.matmul(
                    M.psa[:, SA + cs:SA + 512],
                    M.KT[0:64, hp * T + kb * 128:hp * T + (kb + 1) * 128],
                    M.QT[0:64, hp * T + qt * 512 + cs:hp * T + qt * 512 + 512],
                    start=True, stop=True), t=True)
                tmB = M.pe(lambda e, sm, hp=hp, qt=qt, kb=kb, cs=cs, SB=SB: e.matmul(
                    M.psa[:, SB + cs:SB + 512],
                    M.KT[64:128, hp * T + kb * 128:hp * T + (kb + 1) * 128],
                    M.QT[64:128, hp * T + qt * 512 + cs:hp * T + qt * 512 + 512],
                    start=True, stop=True, tile_position=(64, 0)), t=True)
                pA = par
                pB = 2 + par
                M.wait("act", tmA, tmB, t_P[pA], t_P[pB])
                teA = M.act(lambda e, sm, hp=hp, kb=kb, cs=cs, SA=SA, pA=pA: e.activation(
                    out=M.Pbuf[:, pA * 512 + cs:pA * 512 + 512], in_=M.psa[:, SA + cs:SA + 512],
                    func=AF.Exp, bias=M.btab[:, (2 * hp) * 16 + kb:(2 * hp) * 16 + kb + 1],
                    scale=0.125), t=True)
                teB = M.act(lambda e, sm, hp=hp, kb=kb, cs=cs, SB=SB, pB=pB: e.activation(
                    out=M.Pbuf[:, pB * 512 + cs:pB * 512 + 512], in_=M.psa[:, SB + cs:SB + 512],
                    func=AF.Exp, bias=M.btab[:, (2 * hp + 1) * 16 + kb:(2 * hp + 1) * 16 + kb + 1],
                    scale=0.125), t=True)
                t_S[par * 2], t_S[par * 2 + 1] = teA, teB
                if diag >= 0:
                    M.wait("dve", teA, teB)
                    teA = M.dve(lambda e, sm, cs=cs, pA=pA: e.tensor_tensor(
                        out=M.Pbuf[:, pA * 512 + cs:pA * 512 + cs + 128],
                        in0=M.Pbuf[:, pA * 512 + cs:pA * 512 + cs + 128],
                        in1=M.trim[:], op=ALU.mult), t=True)
                    teB = M.dve(lambda e, sm, cs=cs, pB=pB: e.tensor_tensor(
                        out=M.Pbuf[:, pB * 512 + cs:pB * 512 + cs + 128],
                        in0=M.Pbuf[:, pB * 512 + cs:pB * 512 + cs + 128],
                        in1=M.trim[:], op=ALU.mult), t=True)
                M.wait("pe", teA, teB)
                if kb == 0:
                    M.wait("pe", t_pv[0], t_pv[1], *t_wp)
                va = (kb * 16 + 2 * hp) * 65
                vb = (kb * 16 + 2 * hp + 1) * 65
                tpA = M.pe(lambda e, sm, va=va, cs=cs, pA=pA, kb=kb, last=last: e.matmul(
                    M.psb[0:65, cs:512], M.Vsb[:, va:va + 65], M.Pbuf[:, pA * 512 + cs:pA * 512 + 512],
                    start=(kb == 0), stop=(kb == last)), t=True)
                tpB = M.pe(lambda e, sm, vb=vb, cs=cs, pB=pB, kb=kb, last=last: e.matmul(
                    M.psb[0:65, 512 + cs:1024], M.Vsb[:, vb:vb + 65], M.Pbuf[:, pB * 512 + cs:pB * 512 + 512],
                    start=(kb == 0), stop=(kb == last)), t=True)
                t_P[pA], t_P[pB] = tpA, tpB
            M.wait("dve", tpA, tpB, getattr(M, "t_rsrow", None), getattr(M, "t_anrm", None))
            M.dve(lambda e, sm: e.tensor_copy(
                out=M.anrm[0][0:64, :], in_=M.psb[0:64, 0:512]))
            M.dve(lambda e, sm: e.tensor_copy(
                out=M.anrm[1][0:64, :], in_=M.psb[0:64, 512:1024]))
            M.dve(lambda e, sm: e.tensor_copy(
                out=M.rsrow[32:33, 0:512], in_=M.psb[64:65, 0:512]))
            tnB = M.dve(lambda e, sm: e.tensor_copy(
                out=M.rsrow[64:65, 0:512], in_=M.psb[64:65, 512:1024]), t=True)
            t_pv = [tnB, tnB]
            for sub in range(2):
                M.dve(lambda e, sm, sub=sub: e.reciprocal(
                    out=M.rsrow[96:97, 0:512], in_=M.rsrow[32 + 32 * sub:33 + 32 * sub, 0:512]))
                M.wait("dve", getattr(M, "t_castfree", None))
                trs = M.dve(lambda e, sm: e.tensor_copy(
                    out=M.rsrow[:].bitcast(BF16)[0:1, 0:512], in_=M.rsrow[96:97, 0:512]), t=True)
                M.wait("pe", trs, t_rep)
                trA = M.pe(lambda e, sm: e.matmul(
                    M.psb[0:64, 1024:1536], M.trim[0:1, 0:64],
                    M.rsrow[:].bitcast(BF16)[0:1, 0:512], start=True, stop=True), t=True)
                M.t_castfree = trA
                M.wait("dve", trA)
                tnrm = M.dve(lambda e, sm, hp=hp, sub=sub: e.tensor_tensor(
                    out=M.attnNT[sub * 64:(sub + 1) * 64, hp * 512:(hp + 1) * 512],
                    in0=M.anrm[sub][0:64, :],
                    in1=M.psb[0:64, 1024:1536], op=ALU.mult), t=True)
                t_rep = tnrm
            M.t_anrm = tnrm
            M.t_rsrow = trA
        wcols = [(0, 512), (512, 1024), (1024, 1536)]
        i = 0
        for tsub in range(4):
            M.wait("qin", t_p3res[tsub])
            t_rl = M.dma("qin", M.p3res[tsub][:], M.xd[s, qt * 512 + tsub * 128:qt * 512 + (tsub + 1) * 128, :], sem="s_pr%d" % tsub)
            po = tsub % 2
            evs = []
            for eh in range(2):
                sl_ = i % 3
                i += 1
                c0, c1 = wcols[sl_]
                M.wait("pe", t_wp[sl_])
                if tsub == 0 and eh == 0:
                    M.wait("pe", t_pv[0], t_pv[1], t_rep)
                for hp in range(8):
                    tm = M.pe(lambda e, sm, hp=hp, tsub=tsub, eh=eh, c0=c0, c1=c1: e.matmul(
                        M.psb[:, c0:c1],
                        M.attnNT[:, hp * 512 + tsub * 128:hp * 512 + (tsub + 1) * 128],
                        M.Wstage[:, hp * 1024 + eh * 512:hp * 1024 + eh * 512 + 512],
                        start=(hp == 0), stop=(hp == 7)), t=(hp == 7))
                M.wait("dve", tm, t_rl, t_p3out[po])
                ev = M.dve(lambda e, sm, tsub=tsub, eh=eh, po=po, c0=c0, c1=c1: e.tensor_tensor(
                    out=M.p3out[po][:, eh * 512:eh * 512 + 512], in0=M.psb[:, c0:c1],
                    in1=M.p3res[tsub][:, eh * 512:eh * 512 + 512], op=ALU.add), t=True)
                t_wp[sl_] = ev
                evs.append(ev)
            M.wait("qout", *evs)
            t_out = M.dma("qout", M.out_d[s, qt * 512 + tsub * 128:qt * 512 + (tsub + 1) * 128, :],
                          M.p3out[po][:], sem="s_po%d" % po)
            t_p3out[po] = t_out
            t_p3res[tsub] = t_out
        M.t_res_tok[(s, qt)] = t_out
    M.t_wstage_use = M.tok("pe_sem")


def ffn_phase(M, s):
    t_w1 = [None, None]
    t_w2 = [None, None]
    t_h1free = [None, None]
    t_h1free2 = [None, None]
    t_h2pre = [None, None]
    t_hL = None
    t_res = [None] * 4
    t_psO = [None] * 4
    t_out2 = [None, None]
    t_psb_free = None
    t_y = None
    t_i16f = None
    t_Ic = None
    t_M = None
    for q in range(NQT):
        for cc in range(NCC):
            par = cc % 2
            M.wait("qin", t_w1[par])
            tl = M.dma("qin", M.w1st[par][:], M.w1qT_d[cc, :, :], sem="s_w1%d" % par)
            M.wait("pe", tl, t_h1free[par], t_h1free2[par], t_psb_free)
            c0 = par * 512
            for dc in range(NDC):
                tm = M.pe(lambda e, sm, dc=dc, q=q, c0=c0, par=par: e.matmul(
                    M.psb[:, c0:c0 + 512],
                    M.w1st[par][:, dc * 128:(dc + 1) * 128],
                    M.xqT[:, dc * T + q * 512:dc * T + q * 512 + 512],
                    start=(dc == 0), stop=(dc == NDC - 1)), t=(dc == NDC - 1))
                if q < NQT - 1:
                    tm = M.pe(lambda e, sm, dc=dc, q=q, par=par: e.matmul(
                        M.psb[:, 1024 + par:1025 + par],
                        M.w1st[par][:, dc * 128:(dc + 1) * 128],
                        M.xqT[:, dc * T + q * 512 + 512:dc * T + q * 512 + 513],
                        start=(dc == 0), stop=(dc == NDC - 1)), t=(dc == NDC - 1))
            t_w1[par] = tm
            M.wait("act", tm, t_h2pre[par])
            ta1 = M.act(lambda e, sm, cc=cc, c0=c0, par=par: e.activation(
                out=M.h2pre[par][:], in_=M.psb[:, c0:c0 + 512], func=AF.Identity,
                scale=M.cwb[:, cc * 4 + 1:cc * 4 + 2], bias=M.cwb[:, cc * 4 + 3:cc * 4 + 4]), t=True)
            t_h1free[par] = ta1
            M.wait("dve", ta1, t_hL)
            td = M.dve(lambda e, sm, cc=cc, c0=c0, par=par: e.scalar_tensor_tensor(
                out=M.h2pre[par][:, 1:512], in0=M.psb[:, c0:c0 + 511],
                scalar=M.cwb[:, cc * 4:cc * 4 + 1], in1=M.h2pre[par][:, 1:512],
                op0=ALU.mult, op1=ALU.add))
            td = M.dve(lambda e, sm, cc=cc, c0=c0, par=par: e.scalar_tensor_tensor(
                out=M.h2pre[par][:, 0:511], in0=M.psb[:, c0 + 1:c0 + 512],
                scalar=M.cwb[:, cc * 4 + 2:cc * 4 + 3], in1=M.h2pre[par][:, 0:511],
                op0=ALU.mult, op1=ALU.add), t=(q == NQT - 1))
            if q > 0 and "nohalo" not in M.debug:
                M.dve(lambda e, sm, cc=cc, par=par: e.scalar_tensor_tensor(
                    out=M.h2pre[par][:, 0:1], in0=M.halo[:, cc:cc + 1],
                    scalar=M.cwb[:, cc * 4:cc * 4 + 1], in1=M.h2pre[par][:, 0:1],
                    op0=ALU.mult, op1=ALU.add))
            if q < NQT - 1 and "nohalo" not in M.debug:
                M.dve(lambda e, sm, cc=cc, par=par: e.scalar_tensor_tensor(
                    out=M.h2pre[par][:, 511:512], in0=M.psb[:, 1024 + par:1025 + par],
                    scalar=M.cwb[:, cc * 4 + 2:cc * 4 + 3], in1=M.h2pre[par][:, 511:512],
                    op0=ALU.mult, op1=ALU.add))
                td = M.dve(lambda e, sm, cc=cc, c0=c0: e.tensor_copy(
                    out=M.halo[:, cc:cc + 1], in_=M.psb[:, c0 + 511:c0 + 512]), t=True)
                t_hL = td
            t_h1free2[par] = td
            M.wait("act", td)
            t_h2pre[par] = M.act(lambda e, sm, cc=cc, par=par: e.activation(
                out=M.h2buf[:, cc * 512:(cc + 1) * 512], in_=M.h2pre[par][:], func=AF.Silu), t=True)
        M.wait("dve", t_h2pre[0], t_h2pre[1], t_M)
        t_M = M.dve(lambda e, sm: e.tensor_reduce(
            out=M.Mmax[:], in_=M.h2buf[:].rearrange("p (c t) -> p t c", c=NCC),
            axis=AX.X, op=ALU.max, apply_absolute_value=True), t=True)
        M.wait("pe", t_M, t_h1free[0], t_h1free[1], t_h1free2[0], t_h1free2[1])
        t_tr = []
        for tsub in range(4):
            ttr = M.pe(lambda e, sm, tsub=tsub: e.transpose(
                M.psb[:, tsub * 128:(tsub + 1) * 128], M.Mmax[:, tsub * 128:(tsub + 1) * 128],
                M.identf[:]), t=True)
            t_tr.append(ttr)
        M.wait("dve", *t_tr)
        for tsub in range(4):
            M.dve(lambda e, sm, tsub=tsub: e.tensor_reduce(
                out=M.fcols[:, tsub:tsub + 1], in_=M.psb[:, tsub * 128:(tsub + 1) * 128],
                axis=AX.X, op=ALU.max))
        dve_spacer(M)
        for tsub in range(4):
            t15 = M.dve(lambda e, sm, tsub=tsub: e.tensor_scalar(
                out=M.fcols[:, 8 + tsub:9 + tsub], in0=M.fcols[:, tsub:tsub + 1],
                scalar1=1e-5, scalar2=1.0 / 127.0, op0=ALU.max, op1=ALU.mult), t=(tsub == 3))
        M.wait("act", t15)
        t_hop2 = M.act(lambda e, sm: e.activation(out=M.fcols[:, 16:20], in_=M.fcols[:, 8:12],
                                                  func=AF.Copy), t=True)
        M.wait("dve", t_hop2)
        for tsub in range(4):
            M.dve(lambda e, sm, tsub=tsub: e.reciprocal(
                out=M.fcols[:, 24 + tsub:25 + tsub], in_=M.fcols[:, 16 + tsub:17 + tsub]))
        for tsub in range(4):
            M.dve(lambda e, sm, tsub=tsub: e.tensor_tensor(
                out=M.fcols[:, 40 + tsub:41 + tsub], in0=M.fcols[:, 8 + tsub:9 + tsub],
                in1=M.sca[:, 13:14], op=ALU.mult))
        tfc = dve_spacer(M, t=True)
        M.wait("pe", tfc)
        t_str = []
        for tsub in range(4):
            tt2 = M.pe(lambda e, sm, tsub=tsub: e.transpose(
                M.psb[0:1, 512 + tsub * 128:512 + (tsub + 1) * 128],
                M.fcols[:, 24 + tsub:25 + tsub], M.identf[:]), t=True)
            t_str.append(tt2)
        M.wait("dve", *t_str)
        tsr = M.dve(lambda e, sm: e.tensor_copy(out=M.srowF[0:1, 0:512], in_=M.psb[0:1, 512:1024]), t=True)
        M.wait("pe", tsr)
        trp = M.pe(lambda e, sm: e.matmul(M.psb[:, 1024:1536], M.ones[0:1, 0:128],
                   M.srowF[0:1, 0:512], start=True, stop=True), t=True)
        M.wait("dve", trp)
        t_psb_free = M.dve(lambda e, sm: e.tensor_copy(out=M.repsh[:], in_=M.psb[:, 1024:1536]), t=True)
        for eh in range(2):
            for cc in range(NCC):
                par = cc % 2
                M.wait("qin", t_w2[par])
                tl2 = M.dma("qin", M.w2st[par][:], M.w2qT_d[eh, cc, :, :], sem="s_w2%d" % par)
                if eh == 0:
                    M.wait("dve", t_psb_free, t_y)
                    t_y = M.dve(lambda e, sm, cc=cc: e.tensor_tensor(
                        out=M.yst[:], in0=M.h2buf[:, cc * 512:(cc + 1) * 512], in1=M.repsh[:],
                        op=ALU.mult), t=True)
                    M.wait("act", t_y, t_i16f)
                    t_i16f = M.act(lambda e, sm: e.activation(
                        out=M.i16f[:], in_=M.yst[:], func=AF.Copy, bias=1536.0), t=True)
                    t_Ic = M.act(lambda e, sm, cc=cc: e.activation(
                        out=M.Icache[:, cc * 512:(cc + 1) * 512], in_=M.i16f[:], func=AF.Copy,
                        bias=-1536.0), t=True)
                    t_y = t_i16f
                    M.wait("pe", t_Ic)
                M.wait("pe", tl2)
                if cc == 0:
                    M.wait("pe", *t_psO)
                for tsub in range(4):
                    tm2 = M.pe(lambda e, sm, cc=cc, tsub=tsub, par=par: e.matmul(
                        M.psa[:, tsub * 512:(tsub + 1) * 512],
                        M.Icache[:, cc * 512 + tsub * 128:cc * 512 + (tsub + 1) * 128],
                        M.w2st[par][:], start=(cc == 0), stop=(cc == NCC - 1)), t=(tsub == 3))
                t_w2[par] = tm2
            t_mmend = M.tok("pe_sem")
            for tsub in range(4):
                po = tsub % 2
                if eh == 0:
                    M.wait("qin", t_res[tsub], M.t_res_tok.get((s, q)))
                    t_res[tsub] = M.dma("qin", M.resst[tsub][:],
                                        M.out_d[s, q * 512 + tsub * 128:q * 512 + (tsub + 1) * 128, :], sem="s_rs%d" % tsub)
                M.wait("dve", t_mmend, t_res[tsub], t_out2[po])
                if "ffnonly" in M.debug:
                    ev2 = M.dve(lambda e, sm, tsub=tsub, eh=eh, po=po: e.tensor_scalar(
                        out=M.outst[po][:, eh * 512:eh * 512 + 512],
                        in0=M.psa[:, tsub * 512:(tsub + 1) * 512],
                        scalar1=M.fcols[:, 40 + tsub:41 + tsub], scalar2=None,
                        op0=ALU.mult), t=True)
                else:
                    ev2 = M.dve(lambda e, sm, tsub=tsub, eh=eh, po=po: e.scalar_tensor_tensor(
                        out=M.outst[po][:, eh * 512:eh * 512 + 512], in0=M.psa[:, tsub * 512:(tsub + 1) * 512],
                        scalar=M.fcols[:, 40 + tsub:41 + tsub],
                        in1=M.resst[tsub][:, eh * 512:eh * 512 + 512],
                        op0=ALU.mult, op1=ALU.add), t=True)
                t_psO[tsub] = ev2
                M.wait("qout", ev2)
                t_o = M.dma("qout", M.out_d[s, q * 512 + tsub * 128:q * 512 + (tsub + 1) * 128,
                            eh * 512:eh * 512 + 512], M.outst[po][:, eh * 512:eh * 512 + 512], sem="s_fo%d" % po)
                t_out2[po] = t_o
                if eh == 1:
                    t_res[tsub] = t_o
    M.t_ffn_out = M.tok("qout_sem")


def dump(M, name, ap_fn, shape, dtype=F32):
    d = M.dbg_out(name, shape, dtype)
    if d is None:
        return
    phase_join(M)
    t = M.dma("qout", d[:], ap_fn(), sem="s_dbg")
    M.wait("qout", t)


def build(debug=(), phases=("p0", "ln", "qkv", "attn", "ffn"), nseq=NB):
    M = MK(debug=debug)
    _init_state(M)
    p0parts = tuple(p[3:] for p in phases if p.startswith("p0:"))
    if "p0" in phases or p0parts:
        p0_prep(M, parts=p0parts or ("lora", "means", "w1", "w2"))
        phase_join(M)
        dump(M, "sca", lambda: M.sca[:], [128, 32], F32)
        dump(M, "cwb", lambda: M.cwb[:], [128, NCC * 4], F32)
    for s in range(nseq):
        if "ln" in phases:
            ln_phase(M, s)
            phase_join(M)
            if s == 0:
                dump(M, "xnT", lambda: M.xnT[:], [128, NDC * T], BF16)
                dump(M, "xqT", lambda: M.xqT[:], [128, NDC * T], BF16)
                if "lnstats" in M.debug:
                    for i_ in range(4):
                        dump(M, "lnstats%d" % i_, lambda i_=i_: M.resst[i_][:, 0:64], [128, 64], F32)
        if "qkv" in phases:
            qkv_phase(M, s)
            phase_join(M)
            if s == 0:
                dump(M, "QT", lambda: M.QT[:], [128, 8 * T], BF16)
                dump(M, "KT", lambda: M.KT[:], [128, 8 * T], BF16)
                dump(M, "Vsb", lambda: M.Vsb[:], [128, NTT * H * 65], BF16)
                dump(M, "uTq", lambda: M.uT["q"][:], [128, T], BF16)
        if "attn" in phases:
            attn_phase(M, s)
            phase_join(M)
            if s == 0:
                dump(M, "attnNT", lambda: M.attnNT[:], [128, 8 * 512], BF16)
        if "ffn" in phases:
            ffn_phase(M, s)
            phase_join(M)
            if s == 0:
                dump(M, "h2buf", lambda: M.h2buf[:], [128, NCC * 512], F32)
                dump(M, "Icache", lambda: M.Icache[:], [128, NCC * 512], BF16)
                dump(M, "Mmax", lambda: M.Mmax[:], [128, 512], F32)
                dump(M, "fcols", lambda: M.fcols[:], [128, 64], F32)
                dump(M, "srowF", lambda: M.srowF[:], [128, 512], F32)
                dump(M, "repsh", lambda: M.repsh[:], [128, 512], F32)
    for s_ in list(M.cnt):
        M.wait("qout", M.tok(s_))
    nc = M.finalize()
    return M, nc


def shard_inputs(full, core):
    c = build_consts()
    m = {
        "x": np.ascontiguousarray(full["L0_attn_x"][core * NB:(core + 1) * NB]),
        "Wq": full["L0_attn_Wq"], "Wk": full["L0_attn_Wk"], "Wv": full["L0_attn_Wv"],
        "Wp": full["L0_attn_Wp"],
        "Aq": np.ascontiguousarray(np.asarray(full["L0_attn_Aq"]).reshape(NDC, 128, 8).transpose(1, 0, 2).reshape(128, 64)),
        "Ak": np.ascontiguousarray(np.asarray(full["L0_attn_Ak"]).reshape(NDC, 128, 8).transpose(1, 0, 2).reshape(128, 64)),
        "Av": np.ascontiguousarray(np.asarray(full["L0_attn_Av"]).reshape(NDC, 128, 8).transpose(1, 0, 2).reshape(128, 64)),
        "Bq": full["L0_attn_Bq"], "Bk": full["L0_attn_Bk"], "Bv": full["L0_attn_Bv"],
        "w1": full["L0_ffn_w1"],
        "conv_w": np.ascontiguousarray(
            np.asarray(full["L0_ffn_conv_w"]).reshape(NCC, 128, 3).transpose(1, 0, 2).reshape(128, 96)),
        "conv_b": np.ascontiguousarray(np.asarray(full["L0_ffn_conv_b"]).reshape(NCC, 128).T),
        "w2": full["L0_ffn_w2"],
        "ident_bf": c["ident_bf"], "ident_f": c["ident_f"], "trimask": c["trimask"],
        "btab": c["btab"], "ones_f": c["ones_f"],
    }
    return {k: np.asarray(v) for k, v in m.items()}


N_CORES = 8


def kernel(L0_attn_x, L0_attn_Wq, L0_attn_Aq, L0_attn_Bq, L0_attn_Wk, L0_attn_Ak,
           L0_attn_Bk, L0_attn_Wv, L0_attn_Av, L0_attn_Bv, L0_attn_Wp,
           L0_ffn_w1, L0_ffn_conv_w, L0_ffn_conv_b, L0_ffn_w2):
    x = np.asarray(L0_attn_x)
    c = build_consts()
    shared = {
        "Wq": np.asarray(L0_attn_Wq), "Wk": np.asarray(L0_attn_Wk),
        "Wv": np.asarray(L0_attn_Wv), "Wp": np.asarray(L0_attn_Wp),
        "Aq": np.ascontiguousarray(np.asarray(L0_attn_Aq).reshape(NDC, 128, 8).transpose(1, 0, 2).reshape(128, 64)),
        "Ak": np.ascontiguousarray(np.asarray(L0_attn_Ak).reshape(NDC, 128, 8).transpose(1, 0, 2).reshape(128, 64)),
        "Av": np.ascontiguousarray(np.asarray(L0_attn_Av).reshape(NDC, 128, 8).transpose(1, 0, 2).reshape(128, 64)),
        "Bq": np.asarray(L0_attn_Bq), "Bk": np.asarray(L0_attn_Bk), "Bv": np.asarray(L0_attn_Bv),
        "w1": np.asarray(L0_ffn_w1),
        "conv_w": np.ascontiguousarray(np.asarray(L0_ffn_conv_w).reshape(NCC, 128, 3).transpose(1, 0, 2).reshape(128, 96)),
        "conv_b": np.ascontiguousarray(np.asarray(L0_ffn_conv_b).reshape(NCC, 128).T),
        "w2": np.asarray(L0_ffn_w2),
        "ident_bf": c["ident_bf"], "ident_f": c["ident_f"], "trimask": c["trimask"],
        "btab": c["btab"], "ones_f": c["ones_f"],
    }
    M, nc = build()
    in_maps = []
    for core in range(N_CORES):
        m = dict(shared)
        m["x"] = np.ascontiguousarray(x[core * NB:(core + 1) * NB])
        in_maps.append(m)
    res = run_bass_kernel_spmd(nc, in_maps, core_ids=list(range(N_CORES)))
    out = np.concatenate([np.asarray(r["out"]) for r in res.results], axis=0)
    return out


# version 08 (final)
# speedup vs baseline: 1.1365x; 1.1365x over previous
import numpy as np
import ml_dtypes
import concourse.bass as bass
import concourse.mybir as mybir
from concourse.bass_utils import run_bass_kernel_spmd
from contextlib import ExitStack

F32 = mybir.dt.float32
BF16 = mybir.dt.bfloat16
F16 = mybir.dt.float16
AF = mybir.ActivationFunctionType
ALU = mybir.AluOpType
AX = mybir.AxisListType

NB = 2
T = 2048
E = 1024
H = 16
NTT = 16
NQT = 4
NDC = 8
NCC = 32
DHID = 4096
CEXP = 20.0

ENGS = ("pe", "act", "dve", "qin", "qout")
SEMS = ("qin_sem", "qout_sem", "pe_sem", "act_sem", "dve_sem")


def alibi_slopes():
    import math
    start = 2.0 ** (-2.0 ** (-(math.log2(H) - 3)))
    return np.array([start * start ** i for i in range(H)], dtype=np.float64)


QK_SPREAD = 34.5
KCUT = [min(NTT, int((2.0 * QK_SPREAD + 60.0) / sl) // 128 + 2)
        for sl in alibi_slopes()]


def build_consts():
    c = {}
    c["ident_bf"] = np.eye(128, dtype=np.float16)
    c["ident_f"] = np.eye(128, dtype=np.float32)
    c["trimask"] = (np.arange(128)[:, None] <= np.arange(128)[None, :]).astype(ml_dtypes.bfloat16)
    sl = alibi_slopes()
    btab = np.zeros((128, 256), dtype=np.float64)
    for h in range(H):
        for kb in range(NTT):
            btab[:, h * 16 + kb] = -sl[h] * (kb * 128 + np.arange(128)) - CEXP
    c["btab"] = btab.astype(np.float32)
    c["ones_f"] = np.ones((128, 128), dtype=np.float32)
    return c


class MK:
    def __init__(self, debug=()):
        self.nc = bass.Bass("TRN2", target_bir_lowering=False, detect_race_conditions=False)
        self.ops = {e: [] for e in ENGS}
        self.cnt = {s: 0 for s in SEMS}
        self.wm = {e: {} for e in ENGS}
        self.debug = set(debug)
        self.dbg = {}
        self._ctx = ExitStack()
        self._mk_dram()
        self._mk_sbuf()
        nc = self.nc
        backing = self._ctx.enter_context(
            nc.sbuf_tensor("arena", [128, self.arena_total], mybir.dt.uint8))
        base = nc.lookup_mloc(backing).addr
        for h, off in self._carves:
            nc._tensor(h.name, h.shape, h.dtype, type="SB")
            mloc = nc.lookup_mloc(h)
            mloc.allocated = not nc.target_bir_lowering
            mloc.addr = base + off
        self.psa = self._ctx.enter_context(nc.psum_tensor("psa", [128, 2048], F32))
        self.psb = self._ctx.enter_context(nc.psum_tensor("psb", [128, 1536], F32))
        self.pbf = self._ctx.enter_context(nc.psum_tensor("pbf", [128, 1024], F16))
        self.sems = {s: self._ctx.enter_context(nc.semaphore(s)) for s in SEMS}

    def op(self, eng, fn, inc=None, inc_by=1):
        if inc is not None:
            self.cnt[inc] += inc_by
            def wrapped(e, sems, fn=fn, inc=inc, inc_by=inc_by):
                fn(e, sems).then_inc(sems[inc], inc_by)
            self.ops[eng].append(wrapped)
            return (inc, self.cnt[inc])
        self.ops[eng].append(lambda e, sems, fn=fn: fn(e, sems))
        return None

    def pe(self, fn, t=False):  return self.op("pe", fn, inc="pe_sem" if t else None)
    def act(self, fn, t=False): return self.op("act", fn, inc="act_sem" if t else None)
    def dve(self, fn, t=False): return self.op("dve", fn, inc="dve_sem" if t else None)

    def dsem(self, name):
        if name not in self.cnt:
            self.cnt[name] = 0
            self.sems[name] = self._ctx.enter_context(self.nc.semaphore(name))
        return name

    def dma(self, eng, out, in_, sem=None):
        sem = self.dsem(sem or ("qin_sem" if eng == "qin" else "qout_sem"))
        return self.op(eng, lambda e, s, out=out, in_=in_: e.dma_start(out=out, in_=in_),
                       inc=sem, inc_by=16)

    def wait(self, eng, *toks):
        for tok in toks:
            if tok is None:
                continue
            sem, val = tok
            if self.wm[eng].get(sem, 0) >= val:
                continue
            self.wm[eng][sem] = val
            self.ops[eng].append(lambda e, sems, sem=sem, val=val: e.wait_ge(sems[sem], val))

    def tok(self, sem):
        return (sem, self.cnt[sem])

    def join_all(self):
        return [self.tok(s) for s in list(self.cnt)]

    def _mk_dram(self):
        nc = self.nc
        self.xd = nc.dram_tensor("x", [NB, T, E], F32, kind="ExternalInput").ap()
        self.Wd = {w: nc.dram_tensor(w, [E, E], F32, kind="ExternalInput").ap()
                   for w in ("Wq", "Wk", "Wv", "Wp")}
        self.Ad = {w: nc.dram_tensor("A" + w, [128, 64], F32, kind="ExternalInput").ap() for w in "qkv"}
        self.Bd = {w: nc.dram_tensor("B" + w, [8, E], F32, kind="ExternalInput").ap() for w in "qkv"}
        self.w1d = nc.dram_tensor("w1", [DHID, E], F32, kind="ExternalInput").ap()
        self.cwd = nc.dram_tensor("conv_w", [128, 96], F32, kind="ExternalInput").ap()
        self.cbd = nc.dram_tensor("conv_b", [128, 32], F32, kind="ExternalInput").ap()
        self.w2d = nc.dram_tensor("w2", [E, DHID], F32, kind="ExternalInput").ap()
        for cname, shape, dt in (("ident_bf", [128, 128], F16), ("ident_f", [128, 128], F32),
                                 ("trimask", [128, 128], BF16), ("btab", [128, 256], F32),
                                 ("ones_f", [128, 128], F32)):
            setattr(self, cname + "_d", nc.dram_tensor(cname, shape, dt, kind="ExternalInput").ap())
        self.out_d = nc.dram_tensor("out", [NB, T, E], F32, kind="ExternalOutput").ap()
        scr_kind = "ExternalOutput" if "scr" in self.debug else "Internal"
        self.w1qT_d = nc.dram_tensor("w1qT_scr", [NCC, 128, 1024], F16, kind=scr_kind).ap()
        self.w2qT_d = nc.dram_tensor("w2qT_scr", [2, NCC, 128, 512], F16, kind=scr_kind).ap()

    def dbg_out(self, name, shape, dtype=F32):
        if name in self.debug and name not in self.dbg:
            self.dbg[name] = self.nc.dram_tensor("dbg_" + name, list(shape), dtype,
                                                 kind="ExternalOutput").ap()
        return self.dbg.get(name)

    def _mk_sbuf(self):
        nc = self.nc
        self.arena_total = 205 * 1024
        self._carves = []
        cur = [0]

        def place(name, shape, dtype, at=None):
            per = int(np.prod(shape[1:])) * mybir.dt.size(dtype)
            o = cur[0] if at is None else at
            assert o % 32 == 0 and o + per <= self.arena_total, (name, o, per)
            h = nc.virtual_sbuf_tensor(name, list(shape), dtype)
            self._carves.append((h, o))
            if at is None:
                cur[0] = (o + per + 31) & ~31
            return h

        self.identb = place("identb", [128, 128], F16)
        self.identf = place("identf", [128, 128], F32)
        self.trim = place("trim", [128, 128], BF16)
        self.btab = place("btabs", [128, 256], F32)
        self.ones = place("oness", [128, 128], F32)
        self.cwb = place("cwb", [128, NCC * 4], F32)
        self.sca = place("sca", [128, 32], F32)
        self.spbuf = place("spbuf", [128, 256], F32)
        self.BT = {w: place("B%sT" % w, [128, NDC * 8], F16) for w in "qkv"}
        self.AT = {w: place("A%sT" % w, [128, E], F16) for w in "qkv"}
        self.xqT = place("xqT", [128, NDC * T], F16)
        o0 = self._off(self.xqT)
        self.WL = [place("WL%d" % i, [128, 1024], F32, at=o0 + i * 4096) for i in range(4)]
        self.TA = place("TAp", [128, 1024], F32, at=o0 + 16384)
        self.TB = place("TBp", [128, 1024], F32, at=o0 + 20480)
        self.WQp = [place("WQp%d" % j, [128, 1024], F16, at=o0 + 24576 + j * 2048) for j in range(2)]
        self.WEp = [place("WEp%d" % j, [128, 1024], F16, at=o0 + 28672 + j * 2048) for j in range(2)]
        self.B_OFF = cur[0]
        self.xnT = place("xnT", [128, NDC * T], F16)
        self.C_OFF = cur[0]
        self.QT = place("QT", [128, 8 * T], F16)
        self.KT = place("KT", [128, 8 * T], F16)
        self.Vsb = place("Vsb", [128, NTT * H * 65], BF16)
        self.attnNT = place("attnNT", [128, 8 * 512], F16)
        self.Pbuf = place("Pbuf", [128, 4 * 512], BF16)
        a_off = self._off(self.attnNT)
        self.uT = {w: place("uT" + w, [128, T], F16, at=a_off + i * (T * 2))
                   for i, w in enumerate("qkv")}
        self.C_END = cur[0]
        self.Wstage = place("Wstage", [128, 8192], F16)
        self.DX_OFF = cur[0]
        self.Wtmp = [place("Wtmp0", [128, 512], F32), place("Wtmp1", [128, 512], F32)]
        self.Wcast = place("Wcast", [128, 1024], F16)
        self.D_END = cur[0]
        assert cur[0] <= self.arena_total, cur[0]
        self.rsrow = place("rsrow", [128, 512], F32, at=self._off(self.Wcast))

        o = self._off(self.Wstage)
        def placeE(name, shape, dtype):
            nonlocal o
            per = int(np.prod(shape[1:])) * mybir.dt.size(dtype)
            h = place(name, shape, dtype, at=o)
            o = (o + per + 31) & ~31
            return h
        self.xbuf = [placeE("xbuf0", [128, E], F32), placeE("xbuf1", [128, E], F32)]
        self.xnf = placeE("xnf", [128, E], F32)
        self.xnb = placeE("xnb", [128, E], F16)
        self.xqtok = placeE("xqtok", [128, E], F16)
        self.i16st = placeE("i16st", [128, E], F16)
        self.lnst = [placeE("lnst0", [128, 16], F32), placeE("lnst1", [128, 16], F32)]
        assert o <= self._off(self.Wcast), (o, self._off(self.Wcast))

        o = self.B_OFF
        def placeF(name, shape, dtype):
            nonlocal o
            per = int(np.prod(shape[1:])) * mybir.dt.size(dtype)
            h = place(name, shape, dtype, at=o)
            o = (o + per + 31) & ~31
            return h
        self.h2buf = placeF("h2buf", [128, NCC * 512], F32)
        self.Icache = placeF("Icache", [128, NCC * 512], F16)
        self.w1st = [placeF("w1st0", [128, 1024], F16), placeF("w1st1", [128, 1024], F16)]
        self.w2st = [placeF("w2st0", [128, 512], F16), placeF("w2st1", [128, 512], F16)]
        self.resst = [placeF("resst%d" % i, [128, E], F32) for i in range(4)]
        self.outst = [placeF("outst0", [128, E], F32), placeF("outst1", [128, E], F32)]
        self.yst = placeF("yst", [128, 512], F32)
        self.i16f = placeF("i16f", [128, 512], F16)
        self.h2pre = [placeF("h2pre0", [128, 512], F32), placeF("h2pre1", [128, 512], F32)]
        self.repsh = placeF("repsh", [128, 512], F32)
        self.Mmax = placeF("Mmax", [128, 512], F32)
        self.halo = placeF("halo", [128, NCC], F32)
        self.fcols = placeF("fcols", [128, 64], F32)
        self.srowF = placeF("srowF", [128, 512], F32)
        assert o <= self.C_END, (o, self.C_END)

        o = self.B_OFF
        def placeP3(name, shape, dtype):
            nonlocal o
            per = int(np.prod(shape[1:])) * mybir.dt.size(dtype)
            h = place(name, shape, dtype, at=o)
            o = (o + per + 31) & ~31
            return h
        self.p3res = [placeP3("p3res%d" % i, [128, E], F32) for i in range(4)]
        self.p3out = [placeP3("p3out%d" % i, [128, E], F32) for i in range(2)]
        self.anrm = [placeP3("anrm%d" % i, [128, 512], F32) for i in range(2)]
        self.rnorm = placeP3("rnorm", [128, 512], F32)
        self.sumrow = place("sumrow", [128, 1024], F32, at=self._off(self.anrm[0]))
        assert o <= self.C_OFF + 4096, o

    def _off(self, h):
        for hh, o in self._carves:
            if hh is h:
                return o
        raise KeyError

    def finalize(self):
        nc = self.nc
        sems = self.sems
        ops = self.ops
        with self._ctx:
            with nc.Block() as block:
                block.tensor(lambda e: [f(e, sems) for f in ops["pe"]])
                block.scalar(lambda e: [f(e, sems) for f in ops["act"]])
                block.vector(lambda e: [f(e, sems) for f in ops["dve"]])
                block.sync(lambda e: [f(e, sems) for f in ops["qin"]])
                block.gpsimd(lambda e: [f(e, sems) for f in ops["qout"]])
        return nc

    def psa_(self, c0, c1, p0=0, p1=128):
        return lambda: self.psa[p0:p1, c0:c1]


def lap(f):
    return f



def _init_state(M):
    M.pbf_slot = 0
    M.t_pbf_group = []
    M.t_wtmp = [None, None]
    M.t_wcast = None
    M.t_xbuf = [None] * 2
    M.t_lnst = [None] * 2


def tr_group(M, items, rows=128):
    n = len(items)
    assert n <= 8
    M.wait("pe", *getattr(M, "t_pbf_group", []))
    for j, (src_fn, eng, fn) in enumerate(items):
        tp = M.pe(lambda e, s, j=j, src_fn=src_fn: e.transpose(
            M.pbf[0:rows, j * 128:(j + 1) * 128], src_fn(), M.identb[:]), t=(j == n - 1))
    toks = {}
    engs = []
    for eng in ("dve", "act"):
        idxs = [j for j, it in enumerate(items) if it[1] == eng]
        if not idxs:
            continue
        M.wait(eng, tp)
        for k, j in enumerate(idxs):
            fn = items[j][2]
            tv = M.op(eng, lambda e, s, j=j, fn=fn: fn(e, M.pbf[:, j * 128:(j + 1) * 128]),
                      inc={"dve": "dve_sem", "act": "act_sem"}[eng] if k == len(idxs) - 1 else None)
        toks[eng] = tv
    M.t_pbf_group = [t for t in toks.values()]
    return toks


def dve_spacer(M, t=False):
    M.dve(lambda e, s: e.tensor_copy(out=M.spbuf[:], in_=M.btab[:]))
    return M.dve(lambda e, s: e.tensor_copy(out=M.spbuf[:], in_=M.btab[:]), t=t)


def p0_prep(M, parts=("lora", "means", "w1", "w2")):
    for dst, src in ((M.identb, M.ident_bf_d), (M.identf, M.ident_f_d), (M.trim, M.trimask_d),
                     (M.btab, M.btab_d), (M.ones, M.ones_f_d)):
        t_c = M.dma("qin", dst[:], src[:], sem="s_c")
    M.wait("pe", t_c)
    M.wait("dve", t_c)
    M.wait("act", t_c)
    M.dve(lambda e, s: e.tensor_scalar(out=M.sca[:, 16:17], in0=M.ones[:, 0:1],
                                       scalar1=1e-5, scalar2=None, op0=ALU.mult))
    for i4 in range(4):
        tzf = M.dve(lambda e, s, i4=i4: e.tensor_scalar(
            out=M.Wtmp[0][:, i4 * 128:(i4 + 1) * 128], in0=M.ones[:],
            scalar1=0.0, scalar2=None, op0=ALU.mult), t=(i4 == 3))
    M.t_wtmp[0] = tzf

    for w in ("qkv" if "lora" in parts else ""):
        M.wait("dve", M.t_wtmp[0], M.t_wcast)
        tc = M.tok("dve_sem")
        for half in range(2):
            M.wait("qin", tc)
            tl = M.dma("qin", M.Wtmp[0][0:8, :], M.Bd[w][:, half * 512:(half + 1) * 512], sem="s_wt0")
            M.wait("dve", tl)
            tc = M.dve(lambda e, s, half=half: e.tensor_copy(
                out=M.Wcast[:, half * 512:(half + 1) * 512], in_=M.Wtmp[0][:]), t=True)
        M.t_wtmp[0] = tc
        M.wait("pe", tc)
        toks = tr_group(M, [
            (lambda dc=dc: M.Wcast[:, dc * 128:(dc + 1) * 128], "dve",
             lambda e, p, dc=dc, w=w: e.tensor_copy(out=M.BT[w][:, dc * 8:(dc + 1) * 8], in_=p[:, 0:8]))
            for dc in range(NDC)])
        M.t_wcast = toks["dve"]

    for w in ("qkv" if "lora" in parts else ""):
        M.wait("qin", M.t_wtmp[0])
        tl = M.dma("qin", M.Wtmp[0][:, 0:64], M.Ad[w][:], sem="s_wt0")
        M.wait("dve", tl, M.t_wcast)
        tc = M.dve(lambda e, s: e.tensor_copy(out=M.Wcast[:, 0:64], in_=M.Wtmp[0][:, 0:64]), t=True)
        M.t_wtmp[0] = tc
        M.wait("pe", tc)
        toks = tr_group(M, [
            (lambda dc=dc: M.Wcast[:, dc * 8:(dc + 1) * 8], "act",
             lambda e, p, dc=dc, w=w: e.activation(
                 out=M.AT[w][0:8, dc * 128:(dc + 1) * 128], in_=p[0:8, :], func=AF.Copy, scale=2.0))
            for dc in range(NDC)], rows=8)
        M.t_wcast = toks["act"]

    M.t_WL = [None] * 4

    def wsum(wap, nch, c0):
        for i in range(nch):
            b = i % 4
            M.wait("qin", M.t_WL[b])
            tl = M.dma("qin", M.WL[b][:], wap[i * 128:(i + 1) * 128, :], sem="s_wl%d" % b)
            M.wait("dve", tl)
            tr = M.dve(lambda e, s, c0=c0, i=i, b=b: e.tensor_reduce(
                out=M.cwb[:, c0 + i:c0 + i + 1], in_=M.WL[b][:], axis=AX.X, op=ALU.add,
                apply_absolute_value=True), t=True)
            M.t_WL[b] = tr
        return tr

    if "means" not in parts:
        return
    wsum(M.w1d, NCC, 0)
    w2flat = M.w2d[:].rearrange("e (g c) -> (e g) c", g=4)
    wsum(w2flat, NCC, 64)
    t12 = dve_spacer(M, t=True)
    M.wait("pe", t12)
    tb = M.pe(lambda e, s: e.matmul(M.psb[:, 0:128], M.ones[:, 0:128], M.cwb[:, 0:128],
                                    start=True, stop=True), t=True)
    M.wait("dve", tb)
    M.dve(lambda e, s: e.tensor_reduce(out=M.sca[:, 8:9], in_=M.psb[:, 0:32],
                                       axis=AX.X, op=ALU.add))
    M.dve(lambda e, s: e.tensor_reduce(out=M.sca[:, 9:10], in_=M.psb[:, 64:96],
                                       axis=AX.X, op=ALU.add))
    dve_spacer(M)
    M.dve(lambda e, s: e.tensor_scalar(out=M.sca[:, 12:13], in0=M.sca[:, 8:9],
                                       scalar1=1.0 / (DHID * E), scalar2=1e-5, op0=ALU.mult, op1=ALU.max))
    M.dve(lambda e, s: e.tensor_scalar(out=M.sca[:, 13:14], in0=M.sca[:, 9:10],
                                       scalar1=1.0 / (DHID * E), scalar2=1e-5, op0=ALU.mult, op1=ALU.max))
    dve_spacer(M)
    M.dve(lambda e, s: e.reciprocal(out=M.sca[:, 14:15], in_=M.sca[:, 12:13]))
    t4 = M.dve(lambda e, s: e.reciprocal(out=M.sca[:, 15:16], in_=M.sca[:, 13:14]), t=True)
    dve_spacer(M)

    M.wait("qin", M.t_wtmp[1])
    M.dma("qin", M.Wtmp[1][:, 0:96], M.cwd[:], sem="s_wt1")
    t_cb = M.dma("qin", M.Wtmp[1][:, 96:128], M.cbd[:], sem="s_wt1")
    M.wait("dve", t_cb)
    for cc in range(NCC):
        M.dve(lambda e, s, cc=cc: e.tensor_scalar(
            out=M.cwb[:, cc * 4:cc * 4 + 3], in0=M.Wtmp[1][:, cc * 3:cc * 3 + 3],
            scalar1=M.sca[:, 12:13], scalar2=None, op0=ALU.mult))
        t5 = M.dve(lambda e, s, cc=cc: e.tensor_copy(
            out=M.cwb[:, cc * 4 + 3:cc * 4 + 4], in_=M.Wtmp[1][:, 96 + cc:97 + cc]), t=True)
    M.t_wtmp[1] = t5

    M.t_WQr = [None, None]
    M.t_WEs = [None, None]
    M.t_pbf0 = None
    M.wait("pe", *getattr(M, "t_pbf_group", []))

    def tern_iter(i, src_ap, scol, store_fn):
        b = i % 4
        j = i % 2
        M.wait("qin", M.t_WL[b])
        tl = M.dma("qin", M.WL[b][:], src_ap, sem="s_wl%d" % b)
        M.wait("dve", tl, M.t_WQr[j])
        M.dve(lambda e, s, b=b, scol=scol: e.tensor_scalar(
            out=M.TA[:], in0=M.WL[b][:], scalar1=M.sca[:, scol:scol + 1],
            scalar2=0.5, op0=ALU.mult, op1=ALU.is_ge))
        M.dve(lambda e, s, b=b, scol=scol: e.tensor_scalar(
            out=M.TB[:], in0=M.WL[b][:], scalar1=M.sca[:, scol:scol + 1],
            scalar2=-0.5, op0=ALU.mult, op1=ALU.is_le))
        tq = M.dve(lambda e, s, j=j: e.tensor_tensor(
            out=M.WQp[j][:], in0=M.TA[:], in1=M.TB[:], op=ALU.subtract), t=True)
        M.t_WL[b] = tq
        M.wait("pe", tq, M.t_pbf0)
        for dc in range(NDC):
            tp = M.pe(lambda e, s, dc=dc, j=j: e.transpose(
                M.pbf[:, dc * 128:(dc + 1) * 128], M.WQp[j][:, dc * 128:(dc + 1) * 128],
                M.identb[:]), t=(dc == NDC - 1))
        M.t_WQr[j] = tp
        M.wait("dve", tp, M.t_WEs[j])
        tev = M.dve(lambda e, s, j=j: e.tensor_copy(out=M.WEp[j][:], in_=M.pbf[:, 0:1024]), t=True)
        M.t_pbf0 = tev
        M.wait("qout", tev)
        M.t_WEs[j] = store_fn(j)

    for cc in range(NCC if "w1" in parts else 0):
        tern_iter(cc, M.w1d[cc * 128:(cc + 1) * 128, :], 14,
                  lambda j, cc=cc: M.dma("qout", M.w1qT_d[cc, :, :], M.WEp[j][:], sem="s_we%d" % j))
    for ec in range(8 if "w2" in parts else 0):
        eh = ec // 4
        for sub in range(4):
            def store_w2(j, eh=eh, ec=ec, sub=sub):
                for k in range(8):
                    ts = M.dma("qout", M.w2qT_d[eh, sub * 8 + k, :, (ec % 4) * 128:(ec % 4 + 1) * 128],
                               M.WEp[j][:, k * 128:(k + 1) * 128], sem="s_we%d" % j)
                return ts
            tern_iter(ec * 4 + sub, M.w2d[ec * 128:(ec + 1) * 128, sub * 1024:(sub + 1) * 1024],
                      15, store_w2)


def phase_join(M):
    toks = [M.tok(s) for s in list(M.cnt)]
    for e in ENGS:
        for t in toks:
            M.wait(e, t)


def ln_phase(M, s):
    t_xnf = None
    t_i16 = None
    t_xq = None
    t_xnb = None
    t_xnb_tr = None
    t_xq_tr = None
    for tt in range(NTT):
        b = tt % 2
        M.wait("qin", M.t_xbuf[b])
        tl = M.dma("qin", M.xbuf[b][:], M.xd[s, tt * 128:(tt + 1) * 128, :], sem="s_xb%d" % b)
        st = M.lnst[b]
        M.wait("dve", tl, M.t_lnst[b])
        M.dve(lambda e, sm, b=b, st=st: e.tensor_reduce(
            out=st[:, 0:1], in_=M.xbuf[b][:], axis=AX.X, op=ALU.add))
        M.dve(lambda e, sm, b=b: e.tensor_tensor(
            out=M.psa[:, 0:1024], in0=M.xbuf[b][:], in1=M.xbuf[b][:], op=ALU.mult))
        t_sq = M.dve(lambda e, sm, st=st: e.tensor_reduce(
            out=st[:, 1:2], in_=M.psa[:, 0:1024], axis=AX.X, op=ALU.add), t=True)
        M.wait("act", t_sq)
        M.act(lambda e, sm, st=st: e.activation(out=st[:, 9:10], in_=st[:, 0:1],
              func=AF.Square, scale=1.0 / E))
        t_m = M.act(lambda e, sm, st=st: e.activation(out=st[:, 2:3], in_=st[:, 0:1],
              func=AF.Copy, scale=-1.0 / E), t=True)
        M.wait("dve", t_m)
        t_var = M.dve(lambda e, sm, st=st: e.scalar_tensor_tensor(
            out=st[:, 3:4], in0=st[:, 1:2], scalar=1.0 / E, in1=st[:, 9:10],
            op0=ALU.mult, op1=ALU.subtract), t=True)
        M.wait("act", t_var)
        t_sqr = M.act(lambda e, sm, st=st: e.activation(out=st[:, 10:11], in_=st[:, 3:4],
              func=AF.Sqrt, bias=M.sca[:, 16:17]), t=True)
        M.wait("dve", t_sqr)
        t_rstd = M.dve(lambda e, sm, st=st: e.reciprocal(out=st[:, 4:5], in_=st[:, 10:11]), t=True)
        M.wait("act", t_rstd)
        M.act(lambda e, sm, st=st: e.activation(out=st[:, 5:6], in_=st[:, 4:5],
              func=AF.Copy, scale=st[:, 2:3]))
        t_xn0 = M.act(lambda e, sm, b=b, st=st: e.activation(
            out=M.psa[:, 1024:2048], in_=M.xbuf[b][:], func=AF.Identity,
            scale=st[:, 4:5]), t=True)
        M.t_xbuf[b] = t_xn0
        M.wait("dve", t_xn0, t_xnf)
        t_xn = M.dve(lambda e, sm, st=st: e.tensor_scalar(
            out=M.xnf[:], in0=M.psa[:, 1024:2048], scalar1=st[:, 5:6], scalar2=None,
            op0=ALU.add), t=True)
        t_max = M.dve(lambda e, sm, st=st: e.tensor_reduce(out=st[:, 6:7], in_=M.xnf[:],
              axis=AX.X, op=ALU.max, apply_absolute_value=True), t=True)
        dve_spacer(M)
        t_clip = M.dve(lambda e, sm, st=st: e.tensor_scalar(out=st[:, 8:9], in0=st[:, 6:7],
              scalar1=1e-5, scalar2=1.0 / 127.0, op0=ALU.max, op1=ALU.mult), t=True)
        M.wait("act", t_clip)
        t_hop = M.act(lambda e, sm, st=st: e.activation(out=st[:, 12:13], in_=st[:, 8:9],
              func=AF.Copy), t=True)
        M.wait("dve", t_hop)
        t_sx = M.dve(lambda e, sm, st=st: e.reciprocal(out=st[:, 13:14], in_=st[:, 12:13]), t=True)
        dve_spacer(M)
        t_i16 = M.dve(lambda e, sm, st=st: e.tensor_scalar(
            out=M.i16st[:], in0=M.xnf[:], scalar1=st[:, 13:14], scalar2=1536.0,
            op0=ALU.mult, op1=ALU.add), t=True)
        M.wait("dve", t_xq_tr)
        t_xq = M.dve(lambda e, sm, st=st: e.tensor_scalar(
            out=M.xqtok[:], in0=M.i16st[:], scalar1=-1536.0, scalar2=st[:, 8:9],
            op0=ALU.add, op1=ALU.mult), t=True)
        M.t_lnst[b] = t_xq
        if "lnstats" in M.debug:
            M.dve(lambda e, sm, st=st, tt=tt: e.tensor_copy(
                out=M.resst[tt // 4][:, (tt % 4) * 16:(tt % 4) * 16 + 16], in_=st[:]))
        M.wait("act", t_xn, t_xnb_tr)
        t_xnb = M.act(lambda e, sm: e.activation(out=M.xnb[:], in_=M.xnf[:], func=AF.Copy), t=True)
        t_xnf = t_xnb
        M.wait("pe", t_xnb)
        toks = tr_group(M, [
            (lambda dc=dc: M.xnb[:, dc * 128:(dc + 1) * 128], "act",
             lambda e, p, dc=dc, tt=tt: e.activation(
                 out=M.xnT[:, dc * T + tt * 128:dc * T + (tt + 1) * 128], in_=p[:], func=AF.Copy))
            for dc in range(NDC)])
        t_xnb_tr = toks["act"]
        M.wait("pe", t_xq)
        toks = tr_group(M, [
            (lambda dc=dc: M.xqtok[:, dc * 128:(dc + 1) * 128], "dve",
             lambda e, p, dc=dc, tt=tt: e.tensor_copy(
                 out=M.xqT[:, dc * T + tt * 128:dc * T + (tt + 1) * 128], in_=p[:]))
            for dc in range(NDC)])
        t_xq_tr = toks["dve"]
    M.t_xnT_done = t_xnb_tr
    M.t_xqT_done = t_xq_tr


def prep_WT(M, wname, t_stage_free):
    tv = None
    M.wait("act", t_stage_free)
    for ec in range(8):
        for half in range(2):
            b = half
            M.wait("qin", M.t_wtmp[b])
            tl = M.dma("qin", M.Wtmp[b][:],
                       M.Wd[wname][ec * 128:(ec + 1) * 128, half * 512:(half + 1) * 512], sem="s_wt%d" % b)
            M.wait("dve", tl, M.t_wcast if half == 0 else None)
            tc = M.dve(lambda e, s, b=b, half=half: e.tensor_copy(
                out=M.Wcast[:, half * 512:(half + 1) * 512], in_=M.Wtmp[b][:]), t=True)
            M.t_wtmp[b] = tc
        M.wait("pe", tc)
        toks = tr_group(M, [
            (lambda dc=dc: M.Wcast[:, dc * 128:(dc + 1) * 128], "act",
             lambda e, p, dc=dc, ec=ec: e.activation(
                 out=M.Wstage[:, dc * 1024 + ec * 128:dc * 1024 + (ec + 1) * 128],
                 in_=p[:], func=AF.Copy))
            for dc in range(NDC)])
        tv = toks["act"]
        M.t_wcast = tv
    return tv


def qkv_phase(M, s):
    t_uslot = [None, None]
    ucols = [(0, 512), (512, 1024)]
    i = 0
    t_xn_ok = M.t_xnT_done
    for w in "qkv":
        for tg in range(4):
            sl_ = i % 2
            i += 1
            c0, c1 = ucols[sl_]
            M.wait("pe", t_uslot[sl_], t_xn_ok)
            for dc in range(NDC):
                tm = M.pe(lambda e, sm, w=w, dc=dc, tg=tg, c0=c0, c1=c1: e.matmul(
                    M.psb[0:8, c0:c1], M.BT[w][:, dc * 8:(dc + 1) * 8],
                    M.xnT[:, dc * T + tg * 512:dc * T + tg * 512 + 512],
                    start=(dc == 0), stop=(dc == NDC - 1)), t=(dc == NDC - 1))
            M.wait("dve", tm)
            t_ev = M.dve(lambda e, sm, w=w, tg=tg, c0=c0, c1=c1: e.tensor_copy(
                out=M.uT[w][0:8, tg * 512:tg * 512 + 512], in_=M.psb[0:8, c0:c1]), t=True)
            t_uslot[sl_] = t_ev
    t_u_done = t_ev

    t_stage_free = getattr(M, "t_wstage_use", None)
    t_psa = [None] * 4
    for w, dst in (("Wq", M.QT), ("Wk", M.KT)):
        t_prep = prep_WT(M, w, t_stage_free)
        M.wait("pe", t_prep, t_u_done)
        for ec in range(8):
            M.wait("pe", *t_psa)
            for dc in range(NDC):
                for tg in range(4):
                    M.pe(lambda e, sm, w=w, ec=ec, dc=dc, tg=tg: e.matmul(
                        M.psa[:, tg * 512:(tg + 1) * 512],
                        M.Wstage[:, dc * 1024 + ec * 128:dc * 1024 + (ec + 1) * 128],
                        M.xnT[:, dc * T + tg * 512:dc * T + tg * 512 + 512],
                        start=(dc == 0), stop=False))
            wl = "q" if w == "Wq" else "k"
            for tg in range(4):
                tm = M.pe(lambda e, sm, wl=wl, ec=ec, tg=tg: e.matmul(
                    M.psa[:, tg * 512:(tg + 1) * 512],
                    M.AT[wl][0:8, ec * 128:(ec + 1) * 128],
                    M.uT[wl][0:8, tg * 512:tg * 512 + 512],
                    start=False, stop=True), t=True)
                M.wait("act", tm)
                t_psa[tg] = M.act(lambda e, sm, dst=dst, ec=ec, tg=tg: e.activation(
                    out=dst[:, ec * T + tg * 512:ec * T + tg * 512 + 512],
                    in_=M.psa[:, tg * 512:(tg + 1) * 512], func=AF.Copy), t=True)
        t_stage_free = M.tok("pe_sem")

    t_prep = prep_WT(M, "Wv", t_stage_free)
    M.wait("pe", t_prep)
    t_vslot = [None] * 3
    vcols = [(0, 512), (512, 1024), (1024, 1536)]
    i = 0
    t_ones = None
    for tt in range(NTT):
        for eh in range(2):
            sl_ = i % 3
            i += 1
            c0, c1 = vcols[sl_]
            M.wait("pe", t_vslot[sl_])
            for dc in range(NDC):
                M.pe(lambda e, sm, tt=tt, eh=eh, dc=dc, c0=c0, c1=c1: e.matmul(
                    M.psb[:, c0:c1],
                    M.xnT[:, dc * T + tt * 128:dc * T + (tt + 1) * 128],
                    M.Wstage[:, dc * 1024 + eh * 512:dc * 1024 + eh * 512 + 512],
                    start=(dc == 0), stop=False))
            tm = M.pe(lambda e, sm, tt=tt, eh=eh, c0=c0, c1=c1: e.matmul(
                M.psb[:, c0:c1],
                M.uT["v"][0:8, tt * 128:(tt + 1) * 128],
                M.AT["v"][0:8, eh * 512:eh * 512 + 512],
                start=False, stop=True), t=True)
            M.wait("dve", tm)
            def vout(tt, eh):
                a = tt * 1040 + eh * 520
                return M.Vsb[:, a:a + 520].rearrange("p (h c) -> p h c", h=8)[:, :, 0:64]
            t_vslot[sl_] = M.dve(lambda e, sm, tt=tt, eh=eh, c0=c0, c1=c1: e.tensor_copy(
                out=vout(tt, eh), in_=M.psb[:, c0:c1].rearrange("p (h c) -> p h c", h=8)), t=True)
        t_ones = M.dve(lambda e, sm, tt=tt: e.tensor_copy(
            out=M.Vsb[:, tt * 1040:(tt + 1) * 1040].rearrange("p (h c) -> p h c", h=16)[:, :, 64:65],
            in_=M.ones[:, 0:16].rearrange("p (h c) -> p h c", h=16)), t=True)
    M.t_wstage_use = M.tok("pe_sem")
    M.t_v_done = (t_vslot, t_ones)


def attn_phase(M, s):
    t_prep = prep_WT(M, "Wp", M.t_wstage_use)
    M.wait("pe", t_prep)
    t_S = [None] * 4
    t_P = [None] * 4
    t_pv = [None, None]
    t_rep = None
    t_wp = [None, None, None]
    t_p3res = [None] * 4
    t_p3out = [None, None]
    M.t_res_tok = getattr(M, "t_res_tok", {})
    for qt in range(NQT):
        for hp in range(8):
            KA, KB = KCUT[2 * hp], KCUT[2 * hp + 1]
            lastA = min(4 * qt + 3, KA - 1)
            lastB = min(4 * qt + 3, KB - 1)
            tpA = tpB = None
            for kb in range(min(4 * qt + 4, KB)):
                doA = kb < KA
                par = kb % 2
                diag = kb - 4 * qt
                cs = diag * 128 if diag >= 0 else 0
                SA = (par * 2) * 512
                SB = (par * 2 + 1) * 512
                M.wait("pe", t_S[par * 2] if doA else None, t_S[par * 2 + 1])
                tmA = M.pe(lambda e, sm, hp=hp, qt=qt, kb=kb, cs=cs, SA=SA: e.matmul(
                    M.psa[:, SA + cs:SA + 512],
                    M.KT[0:64, hp * T + kb * 128:hp * T + (kb + 1) * 128],
                    M.QT[0:64, hp * T + qt * 512 + cs:hp * T + qt * 512 + 512],
                    start=True, stop=True), t=True) if doA else None
                tmB = M.pe(lambda e, sm, hp=hp, qt=qt, kb=kb, cs=cs, SB=SB: e.matmul(
                    M.psa[:, SB + cs:SB + 512],
                    M.KT[64:128, hp * T + kb * 128:hp * T + (kb + 1) * 128],
                    M.QT[64:128, hp * T + qt * 512 + cs:hp * T + qt * 512 + 512],
                    start=True, stop=True, tile_position=(64, 0)), t=True)
                pA = par
                pB = 2 + par
                M.wait("act", tmA, tmB, t_P[pA] if doA else None, t_P[pB])
                teA = M.act(lambda e, sm, hp=hp, kb=kb, cs=cs, SA=SA, pA=pA: e.activation(
                    out=M.Pbuf[:, pA * 512 + cs:pA * 512 + 512], in_=M.psa[:, SA + cs:SA + 512],
                    func=AF.Exp, bias=M.btab[:, (2 * hp) * 16 + kb:(2 * hp) * 16 + kb + 1],
                    scale=0.125), t=True) if doA else None
                teB = M.act(lambda e, sm, hp=hp, kb=kb, cs=cs, SB=SB, pB=pB: e.activation(
                    out=M.Pbuf[:, pB * 512 + cs:pB * 512 + 512], in_=M.psa[:, SB + cs:SB + 512],
                    func=AF.Exp, bias=M.btab[:, (2 * hp + 1) * 16 + kb:(2 * hp + 1) * 16 + kb + 1],
                    scale=0.125), t=True)
                if doA:
                    t_S[par * 2] = teA
                t_S[par * 2 + 1] = teB
                if diag >= 0:
                    M.wait("dve", teA, teB)
                    if doA:
                        teA = M.dve(lambda e, sm, cs=cs, pA=pA: e.tensor_tensor(
                            out=M.Pbuf[:, pA * 512 + cs:pA * 512 + cs + 128],
                            in0=M.Pbuf[:, pA * 512 + cs:pA * 512 + cs + 128],
                            in1=M.trim[:], op=ALU.mult), t=True)
                    teB = M.dve(lambda e, sm, cs=cs, pB=pB: e.tensor_tensor(
                        out=M.Pbuf[:, pB * 512 + cs:pB * 512 + cs + 128],
                        in0=M.Pbuf[:, pB * 512 + cs:pB * 512 + cs + 128],
                        in1=M.trim[:], op=ALU.mult), t=True)
                M.wait("pe", teA if doA else None, teB)
                if kb == 0:
                    M.wait("pe", t_pv[0], t_pv[1], *t_wp)
                va = (kb * 16 + 2 * hp) * 65
                vb = (kb * 16 + 2 * hp + 1) * 65
                if doA:
                    tpA = M.pe(lambda e, sm, va=va, cs=cs, pA=pA, kb=kb, lastA=lastA: e.matmul(
                        M.psb[0:65, cs:512], M.Vsb[:, va:va + 65], M.Pbuf[:, pA * 512 + cs:pA * 512 + 512],
                        start=(kb == 0), stop=(kb == lastA)), t=True)
                    t_P[pA] = tpA
                tpB = M.pe(lambda e, sm, vb=vb, cs=cs, pB=pB, kb=kb, lastB=lastB: e.matmul(
                    M.psb[0:65, 512 + cs:1024], M.Vsb[:, vb:vb + 65], M.Pbuf[:, pB * 512 + cs:pB * 512 + 512],
                    start=(kb == 0), stop=(kb == lastB)), t=True)
                t_P[pB] = tpB
            M.wait("dve", tpA, tpB, getattr(M, "t_rsrow", None), getattr(M, "t_anrm", None))
            M.dve(lambda e, sm: e.tensor_copy(
                out=M.sumrow[64:65, 0:512], in_=M.psb[64:65, 0:512]))
            M.dve(lambda e, sm: e.tensor_copy(
                out=M.sumrow[64:65, 512:1024], in_=M.psb[64:65, 512:1024]))
            M.dve(lambda e, sm: e.tensor_copy(
                out=M.anrm[0][0:64, :], in_=M.psb[0:64, 0:512]))
            tnB = M.dve(lambda e, sm: e.tensor_copy(
                out=M.anrm[1][0:64, :], in_=M.psb[0:64, 512:1024]), t=True)
            t_pv = [tnB, tnB]
            for sub in range(2):
                M.wait("pe", tnB, t_rep)
                trA = M.pe(lambda e, sm, sub=sub: e.matmul(
                    M.psb[0:64, 1024:1536], M.ones[64:65, 0:64],
                    M.sumrow[64:65, sub * 512:(sub + 1) * 512],
                    start=True, stop=True), t=True)
                M.wait("dve", trA)
                M.dve(lambda e, sm: e.reciprocal(
                    out=M.rnorm[0:64, :], in_=M.psb[0:64, 1024:1536]))
                tnrm = M.dve(lambda e, sm, hp=hp, sub=sub: e.tensor_tensor(
                    out=M.attnNT[sub * 64:(sub + 1) * 64, hp * 512:(hp + 1) * 512],
                    in0=M.anrm[sub][0:64, :],
                    in1=M.rnorm[0:64, :], op=ALU.mult), t=True)
                t_rep = tnrm
            M.t_anrm = tnrm
            M.t_rsrow = trA
        wcols = [(0, 512), (512, 1024), (1024, 1536)]
        i = 0
        for tsub in range(4):
            M.wait("qin", t_p3res[tsub])
            t_rl = M.dma("qin", M.p3res[tsub][:], M.xd[s, qt * 512 + tsub * 128:qt * 512 + (tsub + 1) * 128, :], sem="s_pr%d" % tsub)
            po = tsub % 2
            evs = []
            for eh in range(2):
                sl_ = i % 3
                i += 1
                c0, c1 = wcols[sl_]
                M.wait("pe", t_wp[sl_])
                if tsub == 0 and eh == 0:
                    M.wait("pe", t_pv[0], t_pv[1], t_rep)
                for hp in range(8):
                    tm = M.pe(lambda e, sm, hp=hp, tsub=tsub, eh=eh, c0=c0, c1=c1: e.matmul(
                        M.psb[:, c0:c1],
                        M.attnNT[:, hp * 512 + tsub * 128:hp * 512 + (tsub + 1) * 128],
                        M.Wstage[:, hp * 1024 + eh * 512:hp * 1024 + eh * 512 + 512],
                        start=(hp == 0), stop=(hp == 7)), t=(hp == 7))
                M.wait("dve", tm, t_rl, t_p3out[po])
                ev = M.dve(lambda e, sm, tsub=tsub, eh=eh, po=po, c0=c0, c1=c1: e.tensor_tensor(
                    out=M.p3out[po][:, eh * 512:eh * 512 + 512], in0=M.psb[:, c0:c1],
                    in1=M.p3res[tsub][:, eh * 512:eh * 512 + 512], op=ALU.add), t=True)
                t_wp[sl_] = ev
                evs.append(ev)
            M.wait("qout", *evs)
            t_out = M.dma("qout", M.out_d[s, qt * 512 + tsub * 128:qt * 512 + (tsub + 1) * 128, :],
                          M.p3out[po][:], sem="s_po%d" % po)
            t_p3out[po] = t_out
            t_p3res[tsub] = t_out
        M.t_res_tok[(s, qt)] = t_out
    M.t_wstage_use = M.tok("pe_sem")


def ffn_phase(M, s):
    t_w1 = [None, None]
    t_w2 = [None, None]
    t_h1free = [None, None]
    t_h1free2 = [None, None]
    t_h2pre = [None, None]
    t_hL = None
    t_res = [None] * 4
    t_psO = [None] * 4
    t_out2 = [None, None]
    t_psb_free = None
    t_y = None
    t_i16f = None
    t_Ic = None
    t_M = None
    for q in range(NQT):
        for cc in range(NCC):
            par = cc % 2
            M.wait("qin", t_w1[par])
            tl = M.dma("qin", M.w1st[par][:], M.w1qT_d[cc, :, :], sem="s_w1%d" % par)
            M.wait("pe", tl, t_h1free[par], t_h1free2[par], t_psb_free)
            c0 = par * 512
            for dc in range(NDC):
                tm = M.pe(lambda e, sm, dc=dc, q=q, c0=c0, par=par: e.matmul(
                    M.psb[:, c0:c0 + 512],
                    M.w1st[par][:, dc * 128:(dc + 1) * 128],
                    M.xqT[:, dc * T + q * 512:dc * T + q * 512 + 512],
                    start=(dc == 0), stop=(dc == NDC - 1)), t=(dc == NDC - 1))
                if q < NQT - 1:
                    tm = M.pe(lambda e, sm, dc=dc, q=q, par=par: e.matmul(
                        M.psb[:, 1024 + par:1025 + par],
                        M.w1st[par][:, dc * 128:(dc + 1) * 128],
                        M.xqT[:, dc * T + q * 512 + 512:dc * T + q * 512 + 513],
                        start=(dc == 0), stop=(dc == NDC - 1)), t=(dc == NDC - 1))
            t_w1[par] = tm
            M.wait("act", tm, t_h2pre[par])
            ta1 = M.act(lambda e, sm, cc=cc, c0=c0, par=par: e.activation(
                out=M.h2pre[par][:], in_=M.psb[:, c0:c0 + 512], func=AF.Identity,
                scale=M.cwb[:, cc * 4 + 1:cc * 4 + 2], bias=M.cwb[:, cc * 4 + 3:cc * 4 + 4]), t=True)
            t_h1free[par] = ta1
            M.wait("dve", ta1, t_hL)
            td = M.dve(lambda e, sm, cc=cc, c0=c0, par=par: e.scalar_tensor_tensor(
                out=M.h2pre[par][:, 1:512], in0=M.psb[:, c0:c0 + 511],
                scalar=M.cwb[:, cc * 4:cc * 4 + 1], in1=M.h2pre[par][:, 1:512],
                op0=ALU.mult, op1=ALU.add))
            td = M.dve(lambda e, sm, cc=cc, c0=c0, par=par: e.scalar_tensor_tensor(
                out=M.h2pre[par][:, 0:511], in0=M.psb[:, c0 + 1:c0 + 512],
                scalar=M.cwb[:, cc * 4 + 2:cc * 4 + 3], in1=M.h2pre[par][:, 0:511],
                op0=ALU.mult, op1=ALU.add), t=(q == NQT - 1))
            if q > 0 and "nohalo" not in M.debug:
                M.dve(lambda e, sm, cc=cc, par=par: e.scalar_tensor_tensor(
                    out=M.h2pre[par][:, 0:1], in0=M.halo[:, cc:cc + 1],
                    scalar=M.cwb[:, cc * 4:cc * 4 + 1], in1=M.h2pre[par][:, 0:1],
                    op0=ALU.mult, op1=ALU.add))
            if q < NQT - 1 and "nohalo" not in M.debug:
                M.dve(lambda e, sm, cc=cc, par=par: e.scalar_tensor_tensor(
                    out=M.h2pre[par][:, 511:512], in0=M.psb[:, 1024 + par:1025 + par],
                    scalar=M.cwb[:, cc * 4 + 2:cc * 4 + 3], in1=M.h2pre[par][:, 511:512],
                    op0=ALU.mult, op1=ALU.add))
                td = M.dve(lambda e, sm, cc=cc, c0=c0: e.tensor_copy(
                    out=M.halo[:, cc:cc + 1], in_=M.psb[:, c0 + 511:c0 + 512]), t=True)
                t_hL = td
            t_h1free2[par] = td
            M.wait("act", td)
            t_h2pre[par] = M.act(lambda e, sm, cc=cc, par=par: e.activation(
                out=M.h2buf[:, cc * 512:(cc + 1) * 512], in_=M.h2pre[par][:], func=AF.Silu), t=True)
        M.wait("dve", t_h2pre[0], t_h2pre[1], t_M)
        t_M = M.dve(lambda e, sm: e.tensor_reduce(
            out=M.Mmax[:], in_=M.h2buf[:].rearrange("p (c t) -> p t c", c=NCC),
            axis=AX.X, op=ALU.max, apply_absolute_value=True), t=True)
        M.wait("pe", t_M, t_h1free[0], t_h1free[1], t_h1free2[0], t_h1free2[1])
        t_tr = []
        for tsub in range(4):
            ttr = M.pe(lambda e, sm, tsub=tsub: e.transpose(
                M.psb[:, tsub * 128:(tsub + 1) * 128], M.Mmax[:, tsub * 128:(tsub + 1) * 128],
                M.identf[:]), t=True)
            t_tr.append(ttr)
        M.wait("dve", *t_tr)
        for tsub in range(4):
            M.dve(lambda e, sm, tsub=tsub: e.tensor_reduce(
                out=M.fcols[:, tsub:tsub + 1], in_=M.psb[:, tsub * 128:(tsub + 1) * 128],
                axis=AX.X, op=ALU.max))
        dve_spacer(M)
        for tsub in range(4):
            t15 = M.dve(lambda e, sm, tsub=tsub: e.tensor_scalar(
                out=M.fcols[:, 8 + tsub:9 + tsub], in0=M.fcols[:, tsub:tsub + 1],
                scalar1=1e-5, scalar2=1.0 / 127.0, op0=ALU.max, op1=ALU.mult), t=(tsub == 3))
        M.wait("act", t15)
        t_hop2 = M.act(lambda e, sm: e.activation(out=M.fcols[:, 16:20], in_=M.fcols[:, 8:12],
                                                  func=AF.Copy), t=True)
        M.wait("dve", t_hop2)
        for tsub in range(4):
            M.dve(lambda e, sm, tsub=tsub: e.reciprocal(
                out=M.fcols[:, 24 + tsub:25 + tsub], in_=M.fcols[:, 16 + tsub:17 + tsub]))
        for tsub in range(4):
            M.dve(lambda e, sm, tsub=tsub: e.tensor_tensor(
                out=M.fcols[:, 40 + tsub:41 + tsub], in0=M.fcols[:, 8 + tsub:9 + tsub],
                in1=M.sca[:, 13:14], op=ALU.mult))
        tfc = dve_spacer(M, t=True)
        M.wait("pe", tfc)
        t_str = []
        for tsub in range(4):
            tt2 = M.pe(lambda e, sm, tsub=tsub: e.transpose(
                M.psb[0:1, 512 + tsub * 128:512 + (tsub + 1) * 128],
                M.fcols[:, 24 + tsub:25 + tsub], M.identf[:]), t=True)
            t_str.append(tt2)
        M.wait("dve", *t_str)
        tsr = M.dve(lambda e, sm: e.tensor_copy(out=M.srowF[0:1, 0:512], in_=M.psb[0:1, 512:1024]), t=True)
        M.wait("pe", tsr)
        trp = M.pe(lambda e, sm: e.matmul(M.psb[:, 1024:1536], M.ones[0:1, 0:128],
                   M.srowF[0:1, 0:512], start=True, stop=True), t=True)
        M.wait("dve", trp)
        t_psb_free = M.dve(lambda e, sm: e.tensor_copy(out=M.repsh[:], in_=M.psb[:, 1024:1536]), t=True)
        for eh in range(2):
            for cc in range(NCC):
                par = cc % 2
                M.wait("qin", t_w2[par])
                tl2 = M.dma("qin", M.w2st[par][:], M.w2qT_d[eh, cc, :, :], sem="s_w2%d" % par)
                if eh == 0:
                    M.wait("dve", t_psb_free, t_y)
                    t_y = M.dve(lambda e, sm, cc=cc: e.tensor_tensor(
                        out=M.yst[:], in0=M.h2buf[:, cc * 512:(cc + 1) * 512], in1=M.repsh[:],
                        op=ALU.mult), t=True)
                    M.wait("act", t_y, t_i16f)
                    t_i16f = M.act(lambda e, sm: e.activation(
                        out=M.i16f[:], in_=M.yst[:], func=AF.Copy, bias=1536.0), t=True)
                    t_Ic = M.act(lambda e, sm, cc=cc: e.activation(
                        out=M.Icache[:, cc * 512:(cc + 1) * 512], in_=M.i16f[:], func=AF.Copy,
                        bias=-1536.0), t=True)
                    t_y = t_i16f
                    M.wait("pe", t_Ic)
                M.wait("pe", tl2)
                if cc == 0:
                    M.wait("pe", *t_psO)
                for tsub in range(4):
                    tm2 = M.pe(lambda e, sm, cc=cc, tsub=tsub, par=par: e.matmul(
                        M.psa[:, tsub * 512:(tsub + 1) * 512],
                        M.Icache[:, cc * 512 + tsub * 128:cc * 512 + (tsub + 1) * 128],
                        M.w2st[par][:], start=(cc == 0), stop=(cc == NCC - 1)), t=(tsub == 3))
                t_w2[par] = tm2
            t_mmend = M.tok("pe_sem")
            for tsub in range(4):
                po = tsub % 2
                if eh == 0:
                    M.wait("qin", t_res[tsub], M.t_res_tok.get((s, q)))
                    t_res[tsub] = M.dma("qin", M.resst[tsub][:],
                                        M.out_d[s, q * 512 + tsub * 128:q * 512 + (tsub + 1) * 128, :], sem="s_rs%d" % tsub)
                M.wait("dve", t_mmend, t_res[tsub], t_out2[po])
                if "ffnonly" in M.debug:
                    ev2 = M.dve(lambda e, sm, tsub=tsub, eh=eh, po=po: e.tensor_scalar(
                        out=M.outst[po][:, eh * 512:eh * 512 + 512],
                        in0=M.psa[:, tsub * 512:(tsub + 1) * 512],
                        scalar1=M.fcols[:, 40 + tsub:41 + tsub], scalar2=None,
                        op0=ALU.mult), t=True)
                else:
                    ev2 = M.dve(lambda e, sm, tsub=tsub, eh=eh, po=po: e.scalar_tensor_tensor(
                        out=M.outst[po][:, eh * 512:eh * 512 + 512], in0=M.psa[:, tsub * 512:(tsub + 1) * 512],
                        scalar=M.fcols[:, 40 + tsub:41 + tsub],
                        in1=M.resst[tsub][:, eh * 512:eh * 512 + 512],
                        op0=ALU.mult, op1=ALU.add), t=True)
                t_psO[tsub] = ev2
                M.wait("qout", ev2)
                t_o = M.dma("qout", M.out_d[s, q * 512 + tsub * 128:q * 512 + (tsub + 1) * 128,
                            eh * 512:eh * 512 + 512], M.outst[po][:, eh * 512:eh * 512 + 512], sem="s_fo%d" % po)
                t_out2[po] = t_o
                if eh == 1:
                    t_res[tsub] = t_o
    M.t_ffn_out = M.tok("qout_sem")


def dump(M, name, ap_fn, shape, dtype=F32):
    d = M.dbg_out(name, shape, dtype)
    if d is None:
        return
    phase_join(M)
    t = M.dma("qout", d[:], ap_fn(), sem="s_dbg")
    M.wait("qout", t)


def build(debug=(), phases=("p0", "ln", "qkv", "attn", "ffn"), nseq=NB):
    M = MK(debug=debug)
    _init_state(M)
    p0parts = tuple(p[3:] for p in phases if p.startswith("p0:"))
    if "p0" in phases or p0parts:
        p0_prep(M, parts=p0parts or ("lora", "means", "w1", "w2"))
        phase_join(M)
        dump(M, "sca", lambda: M.sca[:], [128, 32], F32)
        dump(M, "cwb", lambda: M.cwb[:], [128, NCC * 4], F32)
    for s in range(nseq):
        if "ln" in phases:
            ln_phase(M, s)
            phase_join(M)
            if s == 0:
                dump(M, "xnT", lambda: M.xnT[:], [128, NDC * T], BF16)
                dump(M, "xqT", lambda: M.xqT[:], [128, NDC * T], BF16)
                if "lnstats" in M.debug:
                    for i_ in range(4):
                        dump(M, "lnstats%d" % i_, lambda i_=i_: M.resst[i_][:, 0:64], [128, 64], F32)
        if "qkv" in phases:
            qkv_phase(M, s)
            phase_join(M)
            if s == 0:
                dump(M, "QT", lambda: M.QT[:], [128, 8 * T], BF16)
                dump(M, "KT", lambda: M.KT[:], [128, 8 * T], BF16)
                dump(M, "Vsb", lambda: M.Vsb[:], [128, NTT * H * 65], BF16)
                dump(M, "uTq", lambda: M.uT["q"][:], [128, T], BF16)
        if "attn" in phases:
            attn_phase(M, s)
            phase_join(M)
            if s == 0:
                dump(M, "attnNT", lambda: M.attnNT[:], [128, 8 * 512], BF16)
        if "ffn" in phases:
            ffn_phase(M, s)
            phase_join(M)
            if s == 0:
                dump(M, "h2buf", lambda: M.h2buf[:], [128, NCC * 512], F32)
                dump(M, "Icache", lambda: M.Icache[:], [128, NCC * 512], BF16)
                dump(M, "Mmax", lambda: M.Mmax[:], [128, 512], F32)
                dump(M, "fcols", lambda: M.fcols[:], [128, 64], F32)
                dump(M, "srowF", lambda: M.srowF[:], [128, 512], F32)
                dump(M, "repsh", lambda: M.repsh[:], [128, 512], F32)
    for s_ in list(M.cnt):
        M.wait("qout", M.tok(s_))
    nc = M.finalize()
    return M, nc
N_CORES = 8


def kernel(L0_attn_x, L0_attn_Wq, L0_attn_Aq, L0_attn_Bq, L0_attn_Wk, L0_attn_Ak,
           L0_attn_Bk, L0_attn_Wv, L0_attn_Av, L0_attn_Bv, L0_attn_Wp,
           L0_ffn_w1, L0_ffn_conv_w, L0_ffn_conv_b, L0_ffn_w2):
    x = np.asarray(L0_attn_x)
    c = build_consts()
    shared = {
        "Wq": np.asarray(L0_attn_Wq), "Wk": np.asarray(L0_attn_Wk),
        "Wv": np.asarray(L0_attn_Wv), "Wp": np.asarray(L0_attn_Wp),
        "Aq": np.ascontiguousarray(np.asarray(L0_attn_Aq).reshape(NDC, 128, 8).transpose(1, 0, 2).reshape(128, 64)),
        "Ak": np.ascontiguousarray(np.asarray(L0_attn_Ak).reshape(NDC, 128, 8).transpose(1, 0, 2).reshape(128, 64)),
        "Av": np.ascontiguousarray(np.asarray(L0_attn_Av).reshape(NDC, 128, 8).transpose(1, 0, 2).reshape(128, 64)),
        "Bq": np.asarray(L0_attn_Bq), "Bk": np.asarray(L0_attn_Bk), "Bv": np.asarray(L0_attn_Bv),
        "w1": np.asarray(L0_ffn_w1),
        "conv_w": np.ascontiguousarray(np.asarray(L0_ffn_conv_w).reshape(NCC, 128, 3).transpose(1, 0, 2).reshape(128, 96)),
        "conv_b": np.ascontiguousarray(np.asarray(L0_ffn_conv_b).reshape(NCC, 128).T),
        "w2": np.asarray(L0_ffn_w2),
        "ident_bf": c["ident_bf"], "ident_f": c["ident_f"], "trimask": c["trimask"],
        "btab": c["btab"], "ones_f": c["ones_f"],
    }
    M, nc = build()
    in_maps = []
    for core in range(N_CORES):
        m = dict(shared)
        m["x"] = np.ascontiguousarray(x[core * NB:(core + 1) * NB])
        in_maps.append(m)
    res = run_bass_kernel_spmd(nc, in_maps, core_ids=list(range(N_CORES)))
    out = np.concatenate([np.asarray(r["out"]) for r in res.results], axis=0)
    return out
